# version 12
# baseline (speedup 1.0000x reference)
import numpy as np
import concourse.bass as bass
import concourse.mybir as mybir
from concourse.bass_utils import run_bass_kernel_spmd
from contextlib import ExitStack

F32 = mybir.dt.float32
I32 = mybir.dt.int32
AF = mybir.ActivationFunctionType
ALU = mybir.AluOpType
AX = mybir.AxisListType

T = 2048
D = 1024
II = 3072
NJ = D // 128
NTB = T // 512
NIT = II // 128
NHC = D // 128
QMAX = 32767.0
DBG = False
RG = [[0, 1, 2, 3, 4, 5, 6, 7]]


class Sched:
    def __init__(self):
        self.q = {e: [] for e in "tvsgy"}
        self.cnt = {k: 0 for k in ("t", "v", "s", "g", "qy", "qs", "qg")}
        self.hiw = {e: {} for e in "tvsgy"}

    def _wait(self, e, ev):
        assert ev is not None
        sem, val = ev
        if val <= 0 or self.hiw[e].get(sem, -1) >= val:
            return
        self.hiw[e][sem] = val
        self.q[e].append(("w", sem, val))

    def emit(self, e, fn, waits=(), sig=False):
        for ev in waits:
            self._wait(e, ev)
        if sig:
            self.cnt[e] += 1
            self.q[e].append(("i", fn, e))
            return (e, self.cnt[e])
        self.q[e].append(("i", fn, None))
        return None

    def dma(self, e, out, in_, waits=()):
        for ev in waits:
            self._wait(e, ev)
        qk = {"y": "qy", "s": "qs", "g": "qg"}[e]
        self.cnt[qk] += 16
        self.q[e].append(("d", out, in_, qk))
        return (qk, self.cnt[qk])

    def run(self, e, eng, sems):
        for item in self.q[e]:
            if item[0] == "w":
                eng.wait_ge(sems[item[1]], item[2])
            elif item[0] == "i":
                ins = item[1](eng)
                if item[2] is not None:
                    ins.then_inc(sems[item[2]], 1)
            else:
                eng.dma_start(out=item[1], in_=item[2]).then_inc(sems[item[3]], 16)


def prog(S, dr, sb):
    R1, R2, R3 = sb["R1"], sb["R2"], sb["R3"]
    SNAT, SDT, CONSTS = sb["SNAT"], sb["SDT"], sb["CONSTS"]
    SELUG, WCOL, NCOL = sb["SELUG"], sb["WCOL"], sb["NCOL"]
    BOARD, COLS, SCAL = sb["BOARD"], sb["COLS"], sb["SCAL"]
    ONES, EYE = sb["ONES"], sb["EYE"]
    pp, pq = sb["pp"], sb["pq"]
    V = SCAL

    def seg(t_, a, n):
        return t_[:, a:a + n]

    col_state = {}

    def track(colidx, kind, src_ev, src_ap):
        scratch = COLS[:, 10 + (colidx % 3):11 + (colidx % 3)]
        if kind == "min":
            er = S.emit("v", (lambda sa=src_ap, d=scratch: lambda e: e.tensor_reduce(
                d.rearrange("p (b o) -> p b o", o=1),
                sa.rearrange("p (b k) -> p b k", b=1), AX.X, ALU.min))(),
                waits=[src_ev], sig=True)
        else:
            er = S.emit("v", (lambda sa=src_ap, d=scratch, ab=(kind == "absmax"):
                              lambda e: e.reduce_max(
                d.rearrange("p (b o) -> p b o", o=1),
                sa.rearrange("p (b k) -> p b k", b=1), axis=AX.X,
                apply_absolute_value=ab))(), waits=[src_ev], sig=True)
        if colidx not in col_state:
            col_state[colidx] = True
            return S.emit("v", (lambda c=colidx, d=scratch: lambda e: e.tensor_copy(
                COLS[:, c:c + 1], d))(), waits=[er], sig=True)
        alu = ALU.min if kind == "min" else ALU.max
        return S.emit("v", (lambda c=colidx, d=scratch, a=alu: lambda e: e.tensor_tensor(
            COLS[:, c:c + 1], COLS[:, c:c + 1], d, a))(), waits=[er], sig=True)

    c2s_last = [None]

    def col_to_scalar(ecol, colidx, dst_slot):
        w = [ecol, e_eye] + ([c2s_last[0]] if c2s_last[0] else [])
        e_tp = S.emit("t", (lambda c=colidx: lambda e: e.transpose(
            pp[3][0:1, 0:128], COLS[:, c:c + 1], EYE[:]))(), waits=w, sig=True)
        er = S.emit("v", (lambda ds=dst_slot: lambda e: e.reduce_max(
            V[0:1, ds:ds + 1].rearrange("p (b o) -> p b o", o=1),
            pp[3][0:1, 0:128].rearrange("p (b k) -> p b k", b=1), axis=AX.X))(),
            waits=[e_tp], sig=True)
        c2s_last[0] = er
        return er

    def allreduce(tag, e_ready):
        e_in = S.dma("g", dr[f"cc{tag}i"], V[0:1, 0:8], waits=[e_ready])
        e_ar = S.emit("g", (lambda t_=tag: lambda e: e.collective_compute(
            "AllReduce", ALU.max, ins=[dr[f"cc{t_}i"]], outs=[dr[f"cc{t_}o"]],
            replica_groups=RG))(), waits=[("qg", e_in[1])], sig=True)
        e_out = S.dma("g", V[0:1, 8:16], dr[f"cc{tag}o"], waits=[e_ar])
        return ("qg", e_out[1])

    def board_bcast(waits):
        e_b = S.emit("t", lambda e: e.matmul(pp[3][:, 0:16], ONES[0:1, :], V[0:1, 16:32],
                                             start=True, stop=True),
                     waits=list(waits) + [e_ones], sig=True)
        return S.emit("v", lambda e: e.tensor_copy(BOARD[:, 0:16], pp[3][:, 0:16]),
                      waits=[e_b], sig=True)

    e_wcol = S.dma("y", WCOL[:], dr["wcols"])
    e_eye = S.dma("y", EYE[:], dr["eye"])
    e_ones = S.dma("y", ONES[:], dr["ones"])
    e_selug = S.dma("y", SELUG[:], dr["selug"])
    e_ncol = S.emit("v", lambda e: e.tensor_scalar_mul(NCOL[:, 0:16], WCOL[:, 0:16], -1.0),
                    waits=[("qy", e_wcol[1])], sig=True)
    S.emit("v", lambda e: e.memset(SCAL[:], 0.0))
    S.emit("v", lambda e: e.memset(COLS[:], 0.0))
    S.emit("v", lambda e: e.memset(CONSTS[:, 0:1], 1e-5))
    S.emit("v", lambda e: e.memset(CONSTS[:, 1:2], -0.5))
    S.emit("v", lambda e: e.memset(CONSTS[:, 2:3], 65536.0))
    S.emit("v", lambda e: e.memset(CONSTS[:, 3:4], 12582912.0))
    e_init = S.emit("v", lambda e: e.memset(CONSTS[:, 4:5], -12582912.0), sig=True)

    e_pre = [None] * NIT

    def prepass_up(ot):
        buf = seg(R2, (ot % 4) * 1024, 1024)
        w = [e_pre[ot - 4]] if ot >= 4 else []
        eld = S.dma("s", buf, dr["wu_n"][ot * 128:(ot + 1) * 128, :], waits=w)
        e_pre[ot] = S.emit("v", (lambda b=buf, o=ot: lambda e: e.reduce_max(
            seg(SNAT, o * 32, 32).rearrange("p (b o2) -> p b o2", o2=1),
            b.rearrange("p (b k) -> p b k", k=32), axis=AX.X,
            apply_absolute_value=True))(), waits=[("qs", eld[1])], sig=True)

    XB = [seg(R3, 0, 2048), seg(R3, 2048, 2048)]
    WKA = [[seg(R3, 4096, 2048), seg(R3, 6144, 2048)],
           [seg(R3, 8192, 2048), seg(R3, 10240, 2048)]]
    e_diff = [None] * NJ
    e_xA = [None] * NJ
    for j in range(NJ):
        w = [e_diff[j - 2]] if j >= 2 else []
        e_xA[j] = S.dma("y", XB[j % 2], dr["xT"][j * 128:(j + 1) * 128, :], waits=w)
        wa, wb = WKA[j % 2]
        wx = ("qy", e_xA[j][1])
        S.emit("s", (lambda x=XB[j % 2], a=wa, jj=j: lambda e: e.activation(
            a, x, AF.Exp, bias=0.0, scale=NCOL[:, jj:jj + 1]))(),
            waits=[wx, e_ncol] + ([e_diff[j - 2]] if j >= 2 else []))
        S.emit("s", (lambda a=wa: lambda e: e.activation(a, a, AF.Ln, bias=1.0))())
        S.emit("s", (lambda x=XB[j % 2], b=wb, jj=j: lambda e: e.activation(
            b, x, AF.Exp, bias=0.0, scale=NCOL[:, 8 + jj:9 + jj]))())
        e_ln = S.emit("s", (lambda b=wb: lambda e: e.activation(b, b, AF.Ln, bias=1.0))(),
                      sig=True)
        e_diff[j] = S.emit("v", (lambda a=wa, b=wb, jj=j: lambda e: e.tensor_sub(
            seg(R1, jj * 2048, 2048), a, b))(), waits=[e_ln], sig=True)
        for ot in range(j * 3, min(j * 3 + 3, NIT)):
            prepass_up(ot)
    e_snu = S.emit("v", lambda e: e.tensor_scalar(
        seg(SNAT, 0, 768), seg(SNAT, 0, 768), 1.0 / 7.0, 1e-12, ALU.mult, ALU.max),
        waits=[e_pre[-1]], sig=True)

    SETB = [[seg(R2, 4096 + k * 2048, 2048) for k in range(4)],
            [seg(R2, 12288, 2048), seg(R2, 14336, 2048),
             seg(R3, 4096, 2048), seg(R3, 6144, 2048)]]
    e_r = [None] * NJ
    e_stat = [None] * NJ
    e_xB = [None] * NJ
    e_i = [None] * NJ
    e_sq = [None] * NJ
    for j in range(NJ):
        W0, W1, W2, W3 = SETB[j % 2]
        back = [e_r[j - 2], e_stat[j - 2]] if j >= 2 else []
        guard = [e_diff[NJ - 1]] if j % 2 == 1 else []
        wx_extra = [e_diff[j]] if j < 2 else [e_r[j - 2]]
        e_xB[j] = S.dma("y", XB[j % 2], dr["xT"][j * 128:(j + 1) * 128, :],
                        waits=wx_extra + ([e_diff[NJ - 1]] if j < 2 else []))
        wx = ("qy", e_xB[j][1])
        dj = seg(R1, j * 2048, 2048)
        e_i[j] = S.emit("s", (lambda d=dj, o=W0: lambda e: e.activation(
            o, d, AF.Sigmoid))(), waits=[e_diff[j]] + back, sig=True)
        e_sg = S.emit("s", (lambda x=XB[j % 2], o=W1, jj=j: lambda e: e.activation(
            o, x, AF.Sigmoid, bias=0.0, scale=WCOL[:, 16 + jj:17 + jj]))(),
            waits=[wx] + back, sig=True)
        e_f = S.emit("v", (lambda i=W0, o=W2: lambda e: e.tensor_scalar(
            o, i, -1.0, 1.0, ALU.mult, ALU.add))(),
            waits=[e_i[j]] + back + guard, sig=True)
        S.emit("v", (lambda o=W1: lambda e: e.tensor_scalar_min(o, o, 0.5))(),
               waits=[e_sg])
        S.emit("v", (lambda x=XB[j % 2], o=W3, jj=j: lambda e: e.tensor_scalar(
            o, x, WCOL[:, 16 + jj:17 + jj], 0.0, ALU.mult, ALU.max))(),
            waits=[wx] + guard + back)
        S.emit("v", (lambda a=W3, b=W1: lambda e: e.tensor_add(b, a, b))())
        S.emit("v", (lambda i=W0, g=W1: lambda e: e.tensor_mul(g, i, g))())
        e_h = S.emit("v", (lambda f=W2, v2=W1, o=W0: lambda e: e.tensor_tensor_scan(
            o, f, v2, 0.0, ALU.mult, ALU.add))(), sig=True)
        e_r[j] = S.emit("v", (lambda x=XB[j % 2], h=W0, d=dj: lambda e: e.tensor_add(
            d, x, h))(), sig=True)
        e_sq[j] = S.emit("v", (lambda d=dj, o=W3: lambda e: e.tensor_mul(o, d, d))(),
                         sig=True)
        last = None
        for tb in range(NTB):
            S.emit("t", (lambda d=dj, jj=j, tbb=tb: lambda e: e.matmul(
                pp[tbb][0:1, :], ONES[:, 0:1], d[:, tbb * 512:(tbb + 1) * 512],
                start=(jj == 0), stop=(jj == NJ - 1)))(),
                waits=[e_r[j], e_ones])
            last = S.emit("t", (lambda o=W3, jj=j, tbb=tb: lambda e: e.matmul(
                pq[tbb // 2][0:1, (tbb % 2) * 512:(tbb % 2) * 512 + 512],
                ONES[:, 0:1], o[:, tbb * 512:(tbb + 1) * 512],
                start=(jj == 0), stop=(jj == NJ - 1)))(),
                waits=[e_sq[j]], sig=(tb == NTB - 1))
        e_stat[j] = last

    RW_S = seg(R3, 4096, 2048)
    RW_Q = seg(R3, 6144, 2048)
    e_rowc = []
    for tb in range(NTB):
        e_rowc.append(S.emit("v", (lambda tbb=tb: lambda e: e.tensor_copy(
            RW_S[0:1, tbb * 512:(tbb + 1) * 512], pp[tbb][0:1, :]))(),
            waits=[e_stat[NJ - 1]], sig=True))
        e_rowc.append(S.emit("v", (lambda tbb=tb: lambda e: e.tensor_copy(
            RW_Q[0:1, tbb * 512:(tbb + 1) * 512],
            pq[tbb // 2][0:1, (tbb % 2) * 512:(tbb % 2) * 512 + 512]))(), sig=True))
    e_m = S.emit("v", lambda e: e.tensor_scalar_mul(RW_S[0:1, :], RW_S[0:1, :],
                                                    1.0 / D), waits=[e_rowc[-1]], sig=True)
    e_q_ = S.emit("v", lambda e: e.tensor_scalar_mul(RW_Q[0:1, :], RW_Q[0:1, :], 1.0 / D),
                  waits=[e_m], sig=True)
    RW_M = seg(R3, 0, 2048)
    e_mm = S.emit("v", lambda e: e.tensor_mul(RW_M[0:1, :], RW_S[0:1, :], RW_S[0:1, :]),
                  waits=[e_q_], sig=True)
    e_var = S.emit("v", lambda e: e.tensor_sub(RW_Q[0:1, :], RW_Q[0:1, :],
                                               RW_M[0:1, :]), waits=[e_mm], sig=True)
    e_lnr = S.emit("s", lambda e: e.activation(RW_Q[0:1, :], RW_Q[0:1, :],
                                               AF.Ln, bias=CONSTS[0:1, 0:1]), waits=[e_var], sig=True)
    e_istd = S.emit("s", lambda e: e.activation(RW_Q[0:1, :], RW_Q[0:1, :],
                                                AF.Exp, bias=0.0, scale=CONSTS[0:1, 1:2]),
                    waits=[e_lnr], sig=True)
    e_bc = []
    for tb in range(NTB):
        e_mb = S.emit("t", (lambda tbb=tb: lambda e: e.matmul(
            pp[tbb][:, :], ONES[0:1, :], RW_S[0:1, tbb * 512:(tbb + 1) * 512],
            start=True, stop=True))(), waits=[e_istd, e_m] + e_rowc, sig=True)
        e_ib = S.emit("t", (lambda tbb=tb: lambda e: e.matmul(
            pq[tbb // 2][:, (tbb % 2) * 512:(tbb % 2) * 512 + 512], ONES[0:1, :],
            RW_Q[0:1, tbb * 512:(tbb + 1) * 512],
            start=True, stop=True))(), sig=True)
        e_bc.append((e_mb, e_ib))
    e_appj = [None] * NJ
    e_ysp = [None] * NJ
    e_ymax = None
    for j in range(NJ):
        for tb in range(NTB):
            dj = R1[:, j * 2048 + tb * 512: j * 2048 + (tb + 1) * 512]
            yj = R2[:, j * 2048 + tb * 512: j * 2048 + (tb + 1) * 512]
            S.emit("v", (lambda d=dj, y=yj, tbb=tb: lambda e: e.tensor_sub(
                y, d, pp[tbb][:, :]))(),
                waits=[e_bc[tb][0], e_bc[tb][1], e_snu, e_stat[NJ - 1]])
            e_appj[j] = S.emit("v", (lambda y=yj, tbb=tb: lambda e: e.tensor_mul(
                y, y, pq[tbb // 2][:, (tbb % 2) * 512:(tbb % 2) * 512 + 512]))(),
                sig=True)
        e_ymax = track(0, "absmax", e_appj[j], seg(R2, j * 2048, 2048))
        e_ysp[j] = S.dma("s", dr["y_sp"][j * 128:(j + 1) * 128, :],
                         seg(R2, j * 2048, 2048), waits=[e_appj[j]])
    e_appl = e_appj[NJ - 1]

    e_mx = col_to_scalar(e_ymax, 0, 0)
    wcc1 = allreduce(1, e_mx)
    e_sx = S.emit("v", lambda e: e.tensor_scalar(V[0:1, 0:1], V[0:1, 8:9], 1.0 / QMAX,
                                                 1e-12, ALU.mult, ALU.max),
                  waits=[wcc1], sig=True)
    e_st1 = S.emit("v", lambda e: e.tensor_copy(V[0:1, 31:32], V[0:1, 0:1]),
                   waits=[e_sx], sig=True)
    if DBG:
        S.dma("s", dr["dbgv"][0:1, :], V[0:1, 0:32], waits=[e_st1])
    e_rsx = S.emit("v", lambda e: e.reciprocal(V[0:1, 16:17], V[0:1, 0:1]),
                   waits=[e_sx, e_st1], sig=True)
    if DBG:
        S.dma("s", dr["dbgv"][1:2, :], V[0:1, 0:32], waits=[e_rsx])
    e_board = board_bcast([e_rsx, e_appl])
    B_RSX = BOARD[:, 0:1]

    sTu = R1[0:32, 0:3072]
    rTu = R1[0:32, 3072:6144]
    e_tpu = None
    for ot in range(NIT):
        w = [e_snu, e_appl, e_eye] + ([e_tpu] if e_tpu else [])
        e_t_ = S.emit("t", (lambda o=ot: lambda e: e.transpose(
            pp[2][0:32, 0:128], seg(SNAT, o * 32, 32), EYE[:]))(), waits=w, sig=True)
        e_tpu = S.emit("s", (lambda o=ot: lambda e: e.copy(
            sTu[:, o * 128:(o + 1) * 128], pp[2][0:32, 0:128]))(), waits=[e_t_], sig=True)
    e_rtu = S.emit("v", lambda e: e.reciprocal(rTu, sTu), waits=[e_tpu], sig=True)

    ITMP = seg(R3, 0, 2048).bitcast(I32)
    e_yq = [None] * NJ
    for j in range(NJ):
        yj = seg(R2, j * 2048, 2048)
        e_a = S.emit("v", (lambda y=yj: lambda e: e.tensor_scalar(
            ITMP, y, B_RSX, None, ALU.mult))(),
            waits=[e_board, ("qs", e_ysp[j][1]), e_r[NJ - 1], e_stat[NJ - 1]], sig=True)
        e_yq[j] = S.emit("v", (lambda y=yj: lambda e: e.tensor_copy(y, ITMP))(),
                         waits=[e_a], sig=True)
    e_yq_all = e_yq[NJ - 1]

    RAW = [seg(R3, 0, 1024), seg(R3, 1024, 1024)]
    WD = [seg(R3, 2048, 1024), seg(R3, 3072, 1024)]
    STG = [seg(R3, 4096 + k * 512, 512) for k in range(4)]
    QTMP = seg(R3, 6144, 1024).bitcast(I32)
    PRE = [seg(R3, 7168, 1024), seg(R3, 8192, 1024)]

    dbg_ev = [None]

    def p1_weight(wT_h, sp_h, e_scales, trackers, extra, first_waits, dbg_dump_it=None):
        e_deq = [None] * NIT
        e_spill = {}
        e_evac = {}
        e_last_red = None
        e_grp = {}

        def stage_prep(it):
            w = [e_deq[it - 2]] if it >= 2 else list(first_waits)
            e_raw = S.dma("y", RAW[it % 2].rearrange("p (hc c) -> p hc c", hc=NHC),
                          wT_h.rearrange("(hc p) i -> p hc i", p=128)[
                              :, :, it * 128:(it + 1) * 128], waits=w)
            selw = [e_scales, e_selug] + (list(first_waits) if it == 0 else [e_deq[it - 1]])
            lastsel = None
            for hc in range(NHC):
                S.emit("t", (lambda h=hc, i2=it: lambda e: e.matmul(
                    pq[0][:, h * 128:(h + 1) * 128], SELUG[:, h * 128:(h + 1) * 128],
                    sTu[:, i2 * 128:(i2 + 1) * 128], start=True, stop=True))(), waits=selw)
                lastsel = S.emit("t", (lambda h=hc, i2=it: lambda e: e.matmul(
                    pq[1][:, h * 128:(h + 1) * 128], SELUG[:, h * 128:(h + 1) * 128],
                    rTu[:, i2 * 128:(i2 + 1) * 128], start=True, stop=True))(),
                    sig=(hc == NHC - 1))
            e_q = S.emit("v", (lambda i2=it: lambda e: e.tensor_tensor(
                QTMP, RAW[i2 % 2], pq[1][:, :], ALU.mult))(),
                waits=[lastsel, ("qy", e_raw[1])] + (list(first_waits) if it == 0 else []),
                sig=True)
            e_qf = S.emit("v", (lambda i2=it: lambda e: e.tensor_copy(
                RAW[i2 % 2], QTMP))(), waits=[e_q], sig=True)
            wdw = [e_qf]
            if it >= 2:
                wdw.append(e_grp[(it - 2) * NTB + NTB - 1])
            e_deq[it] = S.emit("v", (lambda i2=it: lambda e: e.tensor_tensor(
                WD[i2 % 2], RAW[i2 % 2], pq[0][:, :], ALU.mult))(), waits=wdw, sig=True)

        def stage_mm(it):
            nonlocal e_last_red
            for tb in range(NTB):
                g = it * NTB + tb
                bank = pp[g % 4]
                mmw = [e_deq[it], e_yq_all]
                if g >= 4:
                    mmw.append(e_evac[g - 4])
                else:
                    mmw += list(first_waits)
                lastmm = None
                for hc in range(NHC):
                    lastmm = S.emit("t", (lambda b=bank, i2=it, tbb=tb, h=hc: lambda e:
                                          e.matmul(
                        b[:, :], WD[i2 % 2][:, h * 128:(h + 1) * 128],
                        R2[:, h * 2048 + tbb * 512: h * 2048 + (tbb + 1) * 512],
                        start=(h == 0), stop=(h == NHC - 1)))(),
                        waits=mmw if hc == 0 else (), sig=(hc == NHC - 1))
                e_grp[g] = lastmm
                stg = STG[g % 4]
                evw = [lastmm]
                if g >= 4:
                    evw.append(("qs", e_spill[g - 4][1]))
                else:
                    evw += list(first_waits)
                e_ev = S.emit("s", (lambda b=bank, o=stg: lambda e: e.copy(o, b[:, :]))(),
                              waits=evw, sig=True)
                e_evac[g] = e_ev
                e_spill[g] = S.dma("s", sp_h[it * 128:(it + 1) * 128,
                                            tb * 512:(tb + 1) * 512], stg, waits=[e_ev])
                for colidx, kind in trackers:
                    e_last_red = track(colidx, kind, e_ev, stg)

        stage_prep(0)
        for it in range(NIT):
            if it + 1 < NIT:
                stage_prep(it + 1)
            stage_mm(it)
            if extra and it < len(extra):
                extra[it]()
        return e_last_red, e_evac[NIT * NTB - 1], e_spill

    def mk_prepass_rows(wn_h, snat_off, n_row_tiles, row_width, first_wait):
        e_prev = [None, None]
        e_last = [None]
        cls = []
        nb = 1024 // 32
        for ot in range(n_row_tiles):
            for c in range(row_width // 1024):
                def go(o=ot, c2=c):
                    buf = PRE[(o * (row_width // 1024) + c2) % 2]
                    par = (o * (row_width // 1024) + c2) % 2
                    w = [e_prev[par]] if e_prev[par] else list(first_wait)
                    eld = S.dma("y", buf, wn_h[o * 128:(o + 1) * 128,
                                               c2 * 1024:(c2 + 1) * 1024], waits=w)
                    er = S.emit("v", (lambda b=buf, o2=o, c3=c2: lambda e: e.reduce_max(
                        seg(SNAT, snat_off + o2 * (row_width // 32) + c3 * nb, nb)
                        .rearrange("p (b o3) -> p b o3", o3=1),
                        b.rearrange("p (b k) -> p b k", k=32), axis=AX.X,
                        apply_absolute_value=True))(), waits=[("qy", eld[1])], sig=True)
                    e_prev[par] = er
                    e_last[0] = er
                cls.append(go)

        def fin():
            tot = n_row_tiles * (row_width // 32)
            return S.emit("v", (lambda: lambda e: e.tensor_scalar(
                seg(SNAT, snat_off, tot), seg(SNAT, snat_off, tot),
                1.0 / 7.0, 1e-12, ALU.mult, ALU.max))(), waits=[e_last[0]], sig=True)
        return cls, fin

    pre_g, fin_g = mk_prepass_rows(dr["wg_n"], 768, NIT, 1024, [e_yq_all])
    pre_d, fin_d = mk_prepass_rows(dr["wd_n"], 1536, NJ, 3072, [e_yq_all])

    p0_done = [e_stat[NJ - 1], e_yq_all, e_rtu]
    e_upred, e_upevac, up_spills = p1_weight(dr["wuT"], dr["sp_up"], e_rtu,
                                             [(2, "absmax")], pre_g, p0_done)
    e_sng = fin_g()

    e_tpg = None
    for ot in range(NIT):
        w = [e_sng, e_upevac, e_eye] + ([e_tpg] if e_tpg else [])
        e_t_ = S.emit("t", (lambda o=ot: lambda e: e.transpose(
            pp[2][0:32, 0:128], seg(SNAT, 768 + o * 32, 32), EYE[:]))(), waits=w, sig=True)
        e_tpg = S.emit("s", (lambda o=ot: lambda e: e.copy(
            sTu[:, o * 128:(o + 1) * 128], pp[2][0:32, 0:128]))(), waits=[e_t_], sig=True)
    e_rtg = S.emit("v", lambda e: e.reciprocal(rTu, sTu), waits=[e_tpg], sig=True)

    if DBG:
        S.dma("s", dr["dbgt"], R1[0:32, 0:6144], waits=[e_rtg])
    e_gred, e_gevac, g_spills = p1_weight(dr["wgT"], dr["sp_g"], e_rtg,
                                          [(3, "max"), (4, "min")], pre_d, [e_upevac],
                                          dbg_dump_it=(4 if DBG else None))
    e_snd = fin_d()

    c2s_last[0] = e_gevac
    e1 = col_to_scalar(e_upred, 2, 0)
    e2 = col_to_scalar(e_gred, 3, 1)
    e_neg = S.emit("v", lambda e: e.tensor_scalar_mul(COLS[:, 4:5], COLS[:, 4:5], -1.0),
                   waits=[e_gred], sig=True)
    e3 = col_to_scalar(e_neg, 4, 2)
    e_pack = S.emit("v", lambda e: e.memset(V[0:1, 3:8], 0.0),
                    waits=[e1, e2, e3], sig=True)
    if DBG:
        S.dma("s", dr["dbgv"][2:3, :], V[0:1, 0:32], waits=[e_pack])
        S.dma("s", dr["dbgc"], COLS[:, 0:16], waits=[e_pack])
    wcc2 = allreduce(2, e_pack)
    vl = [wcc2]

    def vop(fn, extra=()):
        ev = S.emit("v", fn, waits=[vl[0]] + list(extra), sig=True)
        vl[0] = ev
        return ev

    e_sxr = vop(lambda e: e.tensor_copy(V[0:1, 7:8], V[0:1, 31:32]))
    vop(lambda e: e.tensor_mul(V[0:1, 1:2], V[0:1, 8:9], V[0:1, 7:8]))
    e_su = vop(lambda e: e.tensor_scalar(V[0:1, 1:2], V[0:1, 1:2], 1.0 / QMAX,
                                         1e-12, ALU.mult, ALU.max))
    e_rsu = vop(lambda e: e.reciprocal(V[0:1, 2:3], V[0:1, 1:2]))
    vop(lambda e: e.tensor_max(V[0:1, 4:5], V[0:1, 9:10], V[0:1, 10:11]))
    vop(lambda e: e.tensor_mul(V[0:1, 4:5], V[0:1, 4:5], V[0:1, 7:8]))
    e_sg_ = vop(lambda e: e.tensor_scalar(V[0:1, 5:6], V[0:1, 4:5], 1.0 / QMAX,
                                          1e-12, ALU.mult, ALU.max))
    e_rsg = vop(lambda e: e.reciprocal(V[0:1, 6:7], V[0:1, 5:6]))
    vop(lambda e: e.tensor_mul(V[0:1, 3:4], V[0:1, 9:10], V[0:1, 7:8]))
    SI = V[0:1, 11:12].bitcast(I32)
    e_g1 = vop(lambda e: e.tensor_scalar(SI, V[0:1, 3:4], V[0:1, 6:7], None, ALU.mult))
    e_g2 = vop(lambda e: e.tensor_copy(V[0:1, 12:13], SI))
    e_si = S.emit("s", lambda e: e.activation(V[0:1, 13:14], V[0:1, 12:13], AF.Sigmoid,
                                              bias=0.0, scale=V[0:1, 5:6]),
                  waits=[e_g2], sig=True)
    e_s1 = vop(lambda e: e.tensor_scalar(SI, V[0:1, 13:14], 65536.0, None, ALU.mult),
               extra=[e_si])
    vop(lambda e: e.tensor_copy(V[0:1, 13:14], SI))
    vop(lambda e: e.tensor_scalar_min(V[0:1, 13:14], V[0:1, 13:14], 65535.0))
    vop(lambda e: e.tensor_mul(V[0:1, 14:15], V[0:1, 12:13], V[0:1, 13:14]))
    vop(lambda e: e.tensor_mul(V[0:1, 14:15], V[0:1, 14:15], V[0:1, 5:6]))
    vop(lambda e: e.tensor_scalar_mul(V[0:1, 14:15], V[0:1, 14:15], 1.0 / 65536.0))
    e_sg2 = vop(lambda e: e.tensor_scalar(V[0:1, 15:16], V[0:1, 14:15], 1.0 / QMAX,
                                          1e-12, ALU.mult, ALU.max))
    vop(lambda e: e.reciprocal(V[0:1, 21:22], V[0:1, 15:16]))
    vop(lambda e: e.tensor_mul(V[0:1, 16:17], V[0:1, 7:8], V[0:1, 6:7]))
    vop(lambda e: e.tensor_copy(V[0:1, 17:18], V[0:1, 5:6]))
    vop(lambda e: e.tensor_mul(V[0:1, 18:19], V[0:1, 5:6], V[0:1, 21:22]))
    vop(lambda e: e.tensor_scalar_mul(V[0:1, 18:19], V[0:1, 18:19], 1.0 / 65536.0))
    vop(lambda e: e.tensor_mul(V[0:1, 19:20], V[0:1, 7:8], V[0:1, 2:3]))
    e_c6 = vop(lambda e: e.tensor_mul(V[0:1, 20:21], V[0:1, 15:16], V[0:1, 1:2]))
    if DBG:
        S.dma("s", dr["dbgv"][3:4, :], V[0:1, 0:32], waits=[e_c6])
    e_board2 = board_bcast([e_c6])
    B_C1, B_SG, B_C2 = BOARD[:, 0:1], BOARD[:, 1:2], BOARD[:, 2:3]
    B_C3, B_C6 = BOARD[:, 3:4], BOARD[:, 4:5]

    GBUF = [seg(R1, 8192, 2048), seg(R1, 10240, 2048)]
    UBUF = [seg(R1, 12288, 2048), seg(R1, 14336, 2048)]
    SETS2 = [
        [seg(R3, i * 2048, 2048) for i in range(6)],
        [seg(R2, i * 2048, 2048) for i in range(6)],
        [seg(R2, 12288, 2048), seg(R2, 14336, 2048)]
        + [seg(R1, i * 2048, 2048) for i in range(4)],
    ]
    NKT = II // 128
    e_tilev = [None] * NKT
    e_osp = [None] * NKT
    e_gcons = [None] * NKT
    e_ucons = [None] * NKT
    e_omax = None
    p1_done = [e_gevac, e_snd, ("qs", g_spills[NIT * NTB - 1][1])]
    st_g1 = [None] * NKT
    st_g2 = [None] * NKT
    st_s = {}
    st_v = {}

    def p2_red(k):
        nonlocal e_omax
        OB = SETS2[k % 3][5]
        e_omax = track(5, "absmax", st_g2[k], OB)
        e_tilev[k] = e_omax

    st_ld = {}
    e_s1 = [None] * NKT
    e_rA = [None] * NKT
    e_s2 = [None] * NKT
    e_vX = [None] * NKT
    e_s3 = [None] * NKT
    e_rB = [None] * NKT
    e_rCa = [None] * NKT
    e_rCb = [None] * NKT
    e_rDa = [None] * NKT
    e_rDb = [None] * NKT
    CLAMP = 65535.0 / 65536.0
    MAGIC = 12582912.0
    B_MAG = CONSTS[:, 3:4]
    B_NMAG = CONSTS[:, 4:5]

    def p2_ld_g(k):
        wg_ = [e_gcons[k - 2]] if k >= 2 else list(p1_done)
        st_ld[k] = S.dma("y", GBUF[k % 2], dr["sp_g"][k * 128:(k + 1) * 128, :],
                         waits=wg_ + [("qs", g_spills[k * NTB + NTB - 1][1])])

    def p2_ld_u(k):
        wu_ = [e_ucons[k - 2]] if k >= 2 else list(p1_done)
        st_ld[k + 100] = S.dma("y", UBUF[k % 2], dr["sp_up"][k * 128:(k + 1) * 128, :],
                               waits=wu_ + [("qs", up_spills[k * NTB + NTB - 1][1])])

    def p2_s1(k):
        Q1 = SETS2[k % 3][0]
        w1 = [e_rCb[k - 3]] if k >= 3 else list(p1_done)
        e_s1[k] = S.emit("s", (lambda b=GBUF[k % 2], o=Q1: lambda e: e.activation(
            o, b, AF.Identity, bias=B_MAG, scale=B_C1))(),
            waits=[("qy", st_ld[k][1]), e_board2] + w1, sig=True)
        e_gcons[k] = e_s1[k]

    def p2_rA(k):
        Q1, A = SETS2[k % 3][0], SETS2[k % 3][2]
        w1 = [st_g2[k - 3]] if k >= 3 else list(p1_done)
        e_rA[k] = S.emit("v", (lambda i=Q1, o=A, m=-MAGIC: lambda e:
                               e.tensor_scalar_add(o, i, m))(),
                         waits=[e_s1[k]] + w1, sig=True)

    def p2_s2(k):
        A, B = SETS2[k % 3][2], SETS2[k % 3][3]
        e_s2[k] = S.emit("s", (lambda a=A, o=B: lambda e: e.activation(
            o, a, AF.Sigmoid, bias=0.0, scale=B_SG))(), waits=[e_rA[k]], sig=True)

    def p2_vX(k):
        B = SETS2[k % 3][3]
        e_vX[k] = S.emit("v", (lambda o=B, c=CLAMP: lambda e: e.tensor_scalar_min(
            o, o, c))(), waits=[e_s2[k]], sig=True)

    def p2_s3(k):
        Q2, B = SETS2[k % 3][1], SETS2[k % 3][3]
        e_s3[k] = S.emit("s", (lambda i=B, o=Q2: lambda e: e.activation(
            o, i, AF.Identity, bias=B_MAG, scale=CONSTS[:, 2:3]))(),
            waits=[e_vX[k]], sig=True)

    def p2_rB(k):
        Q2, B = SETS2[k % 3][1], SETS2[k % 3][3]
        e_rB[k] = S.emit("v", (lambda i=Q2, o=B, m=-MAGIC: lambda e:
                               e.tensor_scalar_add(o, i, m))(),
                         waits=[e_s3[k]], sig=True)

    def p2_g1(k):
        A, B, C = SETS2[k % 3][2], SETS2[k % 3][3], SETS2[k % 3][4]
        w1 = [e_rCa[k - 3]] if k >= 3 else list(p1_done)
        st_g1[k] = S.emit("g", (lambda a=A, b=B, o=C: lambda e: e.tensor_mul(
            o, a, b))(), waits=[e_rA[k], e_rB[k]] + w1, sig=True)

    def p2_rCDa(k):
        Q1, Q2, C = SETS2[k % 3][0], SETS2[k % 3][1], SETS2[k % 3][4]
        e_rCa[k] = S.emit("s", (lambda c=C, o=Q1: lambda e: e.activation(
            o, c, AF.Identity, bias=B_MAG, scale=B_C2))(),
            waits=[st_g1[k]], sig=True)
        e_rDa[k] = S.emit("s", (lambda b=UBUF[k % 2], o=Q2: lambda e: e.activation(
            o, b, AF.Identity, bias=B_MAG, scale=B_C3))(),
            waits=[("qy", st_ld[k + 100][1]), e_board2, e_rB[k]], sig=True)
        e_ucons[k] = e_rDa[k]

    def p2_rCDb(k):
        Q1, Q2 = SETS2[k % 3][0], SETS2[k % 3][1]
        A, B = SETS2[k % 3][2], SETS2[k % 3][3]
        e_rCb[k] = S.emit("v", (lambda i=Q1, o=A, m=-MAGIC: lambda e:
                                e.tensor_scalar_add(o, i, m))(),
                          waits=[e_rCa[k]], sig=True)
        e_rDb[k] = S.emit("v", (lambda i=Q2, o=B, m=-MAGIC: lambda e:
                                e.tensor_scalar_add(o, i, m))(),
                          waits=[e_rDa[k], st_g1[k]], sig=True)

    def p2_g2(k):
        A, B, OB = SETS2[k % 3][2], SETS2[k % 3][3], SETS2[k % 3][5]
        back = [e_tilev[k - 3], ("qs", e_osp[k - 3][1])] if k >= 3 else list(p1_done)
        st_g2[k] = S.emit("g", (lambda a=A, b=B, o=OB: lambda e: e.tensor_mul(
            o, a, b))(), waits=[e_rCb[k], e_rDb[k]] + back, sig=True)
        e_osp[k] = S.dma("s", dr["sp_o"][k * 128:(k + 1) * 128, :], OB,
                         waits=[st_g2[k]])

    p2_ld_g(0)
    p2_ld_u(0)
    p2_ld_g(1)
    p2_ld_u(1)
    for k in range(NKT + 3):
        if 1 <= k < NKT + 1:
            p2_s3(k - 1)
        if k < NKT:
            p2_s1(k)
        if 1 <= k < NKT + 1:
            p2_rB(k - 1)
        if k < NKT:
            p2_rA(k)
            p2_s2(k)
        if 2 <= k < NKT + 2:
            p2_rCDa(k - 2)
            p2_rCDb(k - 2)
        if k < NKT:
            p2_vX(k)
        if 3 <= k:
            p2_red(k - 3)
        if 1 <= k < NKT + 1:
            p2_g1(k - 1)
        if 2 <= k < NKT + 2:
            p2_g2(k - 2)
        if k + 2 < NKT:
            p2_ld_g(k + 2)
        if 2 <= k and k < NKT:
            p2_ld_u(k)

    e_mo = col_to_scalar(e_omax, 5, 0)
    wcc3 = allreduce(3, e_mo)
    vl[0] = wcc3
    vop(lambda e: e.tensor_mul(V[0:1, 1:2], V[0:1, 8:9], V[0:1, 20:21]))
    e_so = vop(lambda e: e.tensor_scalar(V[0:1, 1:2], V[0:1, 1:2], 1.0 / QMAX,
                                         1e-12, ALU.mult, ALU.max))
    e_rso = vop(lambda e: e.reciprocal(V[0:1, 2:3], V[0:1, 1:2]))
    e_c7 = vop(lambda e: e.tensor_mul(V[0:1, 3:4], V[0:1, 20:21], V[0:1, 2:3]))
    vop(lambda e: e.tensor_copy(V[0:1, 16:17], V[0:1, 3:4]))
    e_pk3 = vop(lambda e: e.tensor_copy(V[0:1, 17:18], V[0:1, 1:2]))
    if DBG:
        S.dma("s", dr["dbgv"][4:5, :], V[0:1, 0:32], waits=[e_pk3])
    e_board3 = board_bcast([e_pk3])
    B_C7, B_SO = BOARD[:, 0:1], BOARD[:, 1:2]

    sdT = SDT[0:96, 0:1024]
    rdT = SDT[0:96, 1024:2048]
    e_tpd = None
    for ot in range(NJ):
        w = [e_snd, e_eye] + ([e_tpd] if e_tpd else [e_omax])
        e_t_ = S.emit("t", (lambda o=ot: lambda e: e.transpose(
            pp[2][0:96, 0:128], seg(SNAT, 1536 + o * 96, 96), EYE[:]))(), waits=w, sig=True)
        e_tpd = S.emit("s", (lambda o=ot: lambda e: e.copy(
            sdT[:, o * 128:(o + 1) * 128], pp[2][0:96, 0:128]))(), waits=[e_t_], sig=True)
    e_rtd = S.emit("v", lambda e: e.reciprocal(rdT, sdT), waits=[e_tpd], sig=True)

    e_seldn = S.dma("y", R2[0:96, 12288:15360], dr["seldn"],
                    waits=[e_gevac, e_tilev[NKT - 1]])
    OQ = [seg(R2, c * 2048, 2048) for c in range(6)]
    SELDN = R2[0:96, 12288:15360]
    DRAW = [seg(R3, 0, 768), seg(R3, 768, 768)]
    DWD = [seg(R3, 1536, 768), seg(R3, 2304, 768)]
    DQT = seg(R3, 3072, 768).bitcast(I32)
    OIT2 = [seg(R3, 3840, 2048).bitcast(I32), seg(R3, 7936, 2048).bitcast(I32)]
    YP = [seg(R3, 5888, 512), seg(R3, 6400, 512)]
    OST = [seg(R3, 6912, 512), seg(R3, 7424, 512)]

    NPASS, ICB = 4, 6
    e_oq = {}
    e_deqd = {}
    e_z = {}
    zev_order = []
    e_yp_cons = {}
    st_d = {}

    def p3_oq_loads(p):
        for c in range(ICB):
            ic = p * ICB + c
            w = [e_board3, ("qs", e_osp[ic][1])]
            if p > 0:
                w.append(e_z[(p - 1, NJ - 1, NTB - 1)])
            else:
                w.append(e_gevac)
            eld = S.dma("y", OQ[c], dr["sp_o"][ic * 128:(ic + 1) * 128, :], waits=w)
            prev = e_oq.get(ic - 2)
            oit = OIT2[c % 2]
            ei = S.emit("s", (lambda b=OQ[c], o=oit: lambda e: e.activation(
                o, b, AF.Identity, bias=0.0, scale=B_C7))(),
                waits=[("qy", eld[1])] + ([prev] if prev else [e_board3]), sig=True)
            e_oq[ic] = S.emit("v", (lambda b=OQ[c], o=oit: lambda e: e.tensor_copy(
                b, o))(), waits=[ei], sig=True)

    def p3_prep(u):
        p, ho = divmod(u, NJ)
        hb = u % 2
        wsl = [e_deqd[u - 2]] if u >= 2 else [e_rtd]
        eslab = S.dma("y", DRAW[hb].rearrange("p (s c) -> p s c", s=ICB),
                      dr["wdT"].rearrange("(s p) i -> p s i", p=128)[
                          :, p * ICB:(p + 1) * ICB, ho * 128:(ho + 1) * 128],
                      waits=wsl)
        selw = [e_rtd, ("qy", e_seldn[1])] + ([e_deqd[u - 1]] if u >= 1 else [])
        lastsel = None
        for c in range(ICB):
            ic = p * ICB + c
            S.emit("t", (lambda i2=ic, c2=c, h2=ho: lambda e: e.matmul(
                pq[0][:, c2 * 128:(c2 + 1) * 128], SELDN[:, i2 * 128:(i2 + 1) * 128],
                sdT[:, h2 * 128:(h2 + 1) * 128], start=True, stop=True))(), waits=selw)
            lastsel = S.emit("t", (lambda i2=ic, c2=c, h2=ho: lambda e: e.matmul(
                pq[1][:, c2 * 128:(c2 + 1) * 128], SELDN[:, i2 * 128:(i2 + 1) * 128],
                rdT[:, h2 * 128:(h2 + 1) * 128], start=True, stop=True))(),
                sig=(c == ICB - 1))
        e_q = S.emit("v", (lambda b=hb: lambda e: e.tensor_tensor(
            DQT, DRAW[b], pq[1][:, 0:768], ALU.mult))(),
            waits=[lastsel, ("qy", eslab[1])], sig=True)
        e_qf = S.emit("v", (lambda b=hb: lambda e: e.tensor_copy(DRAW[b], DQT))(),
                      waits=[e_q], sig=True)
        dww = [e_qf]
        if u >= 2:
            pv = divmod(u - 2, NJ)
            dww.append(e_z[(pv[0], pv[1], NTB - 1)])
        e_deqd[u] = S.emit("v", (lambda b=hb: lambda e: e.tensor_tensor(
            DWD[b], DRAW[b], pq[0][:, 0:768], ALU.mult))(), waits=dww, sig=True)

    def p3_mm(u):
        p, ho = divmod(u, NJ)
        hb = u % 2
        e_oq_p = e_oq[p * ICB + ICB - 1]
        for tb in range(NTB):
            gi = len(zev_order)
            bank = pp[gi % 2]
            mmw = [e_deqd[u], e_oq_p]
            if gi >= 2:
                mmw.append(zev_order[gi - 2])
            lastmm = None
            for c in range(ICB):
                lastmm = S.emit("t", (lambda b=bank, b2=hb, c2=c, t2=tb: lambda e:
                                      e.matmul(
                    b[:, :], DWD[b2][:, c2 * 128:(c2 + 1) * 128],
                    OQ[c2][:, t2 * 512:(t2 + 1) * 512],
                    start=(c2 == 0), stop=(c2 == ICB - 1)))(),
                    waits=mmw if c == 0 else (), sig=(c == ICB - 1))
            zsl = R1[:, ho * 2048 + tb * 512: ho * 2048 + (tb + 1) * 512]
            if p == 0:
                ez = S.emit("v", (lambda z=zsl, b=bank: lambda e: e.tensor_copy(
                    z, b[:, :]))(), waits=[lastmm, e_omax,
                                           ("qs", e_osp[NKT - 1][1])], sig=True)
            elif p < NPASS - 1:
                ez = S.emit("v", (lambda z=zsl, b=bank: lambda e: e.tensor_add(
                    z, z, b[:, :]))(), waits=[lastmm], sig=True)
            else:
                yw = [e_yp_cons[gi % 2]] if (gi % 2) in e_yp_cons else []
                eyp = S.dma("y", YP[gi % 2],
                            dr["y_sp"][ho * 128:(ho + 1) * 128,
                                       tb * 512:(tb + 1) * 512],
                            waits=yw + [("qs", e_ysp[ho][1])])
                S.emit("v", (lambda z=zsl, b=bank: lambda e: e.tensor_add(
                    z, z, b[:, :]))(), waits=[lastmm])
                ez = S.emit("v", (lambda z=zsl, yp=YP[gi % 2]: lambda e:
                                  e.scalar_tensor_tensor(
                    z, z, B_SO, yp, ALU.mult, ALU.add))(),
                    waits=[("qy", eyp[1]), e_board3], sig=True)
                e_yp_cons[gi % 2] = ez
            e_z[(p, ho, tb)] = ez
            zev_order.append(ez)

    for p in range(NPASS):
        p3_oq_loads(p)
        if p == 0:
            p3_prep(0)
        for ho in range(NJ):
            u = p * NJ + ho
            if u + 1 < NPASS * NJ:
                p3_prep(u + 1)
            p3_mm(u)
    e_z_last = zev_order[-1]

    SQ2 = seg(R3, 0, 2048)
    e_stat2 = None
    for j in range(NJ):
        zj = seg(R1, j * 2048, 2048)
        sqw = [e_z_last] + ([e_stat2] if e_stat2 else [])
        e_sq = S.emit("s", (lambda z=zj, o=SQ2: lambda e: e.activation(
            o, z, AF.Square))(), waits=sqw, sig=True)
        for tb in range(NTB):
            S.emit("t", (lambda z=zj, jj=j, tbb=tb: lambda e: e.matmul(
                pp[tbb][0:1, :], ONES[:, 0:1], z[:, tbb * 512:(tbb + 1) * 512],
                start=(jj == 0), stop=(jj == NJ - 1)))(), waits=[e_z_last])
            e_stat2 = S.emit("t", (lambda o=SQ2, jj=j, tbb=tb: lambda e: e.matmul(
                pq[tbb // 2][0:1, (tbb % 2) * 512:(tbb % 2) * 512 + 512],
                ONES[:, 0:1], o[:, tbb * 512:(tbb + 1) * 512],
                start=(jj == 0), stop=(jj == NJ - 1)))(),
                waits=[e_sq], sig=(tb == NTB - 1))
    RW2_S = seg(R3, 8192, 2048)
    RW2_Q = seg(R3, 10240, 2048)
    e_rowc2 = []
    for tb in range(NTB):
        e_rowc2.append(S.emit("v", (lambda tbb=tb: lambda e: e.tensor_copy(
            RW2_S[0:1, tbb * 512:(tbb + 1) * 512], pp[tbb][0:1, :]))(),
            waits=[e_stat2], sig=True))
        e_rowc2.append(S.emit("v", (lambda tbb=tb: lambda e: e.tensor_copy(
            RW2_Q[0:1, tbb * 512:(tbb + 1) * 512],
            pq[tbb // 2][0:1, (tbb % 2) * 512:(tbb % 2) * 512 + 512]))(), sig=True))
    e_m2 = S.emit("v", lambda e: e.tensor_scalar_mul(RW2_S[0:1, :], RW2_S[0:1, :],
                                                     1.0 / D), waits=[e_rowc2[-1]], sig=True)
    e_q2_ = S.emit("v", lambda e: e.tensor_scalar_mul(RW2_Q[0:1, :], RW2_Q[0:1, :], 1.0 / D),
                   waits=[e_m2], sig=True)
    e_mm2 = S.emit("v", lambda e: e.tensor_mul(SQ2[0:1, :], RW2_S[0:1, :], RW2_S[0:1, :]),
                   waits=[e_q2_], sig=True)
    e_var2 = S.emit("v", lambda e: e.tensor_sub(RW2_Q[0:1, :], RW2_Q[0:1, :],
                                                SQ2[0:1, :]), waits=[e_mm2], sig=True)
    e_lnr2 = S.emit("s", lambda e: e.activation(RW2_Q[0:1, :], RW2_Q[0:1, :],
                                                AF.Ln, bias=CONSTS[0:1, 0:1]), waits=[e_var2], sig=True)
    e_istd2 = S.emit("s", lambda e: e.activation(RW2_Q[0:1, :], RW2_Q[0:1, :],
                                                 AF.Exp, bias=0.0, scale=CONSTS[0:1, 1:2]),
                     waits=[e_lnr2], sig=True)
    e_outs = []
    n_out = 0
    for tb in range(NTB):
        e_mb = S.emit("t", (lambda tbb=tb: lambda e: e.matmul(
            pp[tbb][:, :], ONES[0:1, :], RW2_S[0:1, tbb * 512:(tbb + 1) * 512],
            start=True, stop=True))(), waits=[e_istd2, e_m2] + e_rowc2, sig=True)
        e_ib = S.emit("t", (lambda tbb=tb: lambda e: e.matmul(
            pq[tbb // 2][:, (tbb % 2) * 512:(tbb % 2) * 512 + 512], ONES[0:1, :],
            RW2_Q[0:1, tbb * 512:(tbb + 1) * 512],
            start=True, stop=True))(), sig=True)
        for j in range(NJ):
            zj = R1[:, j * 2048 + tb * 512: j * 2048 + (tb + 1) * 512]
            ost = OST[n_out % 2]
            w_st = [("qs", e_outs[n_out - 2][1])] if n_out >= 2 else []
            S.emit("v", (lambda z=zj, o=ost, tbb=tb: lambda e: e.tensor_sub(
                o, z, pp[tbb][:, :]))(), waits=[e_mb, e_ib] + w_st)
            e_o = S.emit("v", (lambda o=ost, tbb=tb: lambda e: e.tensor_mul(
                o, o, pq[tbb // 2][:, (tbb % 2) * 512:(tbb % 2) * 512 + 512]))(), sig=True)
            e_outs.append(S.dma("s", dr["out"][j * 128:(j + 1) * 128,
                                               tb * 512:(tb + 1) * 512], ost, waits=[e_o]))
            n_out += 1
    if DBG:
        S.dma("s", dr["dbgs"], SNAT[:, 0:2304], waits=[e_outs[-1]])
        for j in range(NJ):
            S.dma("s", dr["dbgz"][j * 128:(j + 1) * 128, :], seg(R1, j * 2048, 2048),
                  waits=[e_outs[-1]])
    S.emit("s", lambda e: e.nop(), waits=[("qs", S.cnt["qs"])])
    S.emit("g", lambda e: e.nop(), waits=[("qs", e_outs[-1][1])])


def build_nc():
    nc = bass.Bass("TRN2", target_bir_lowering=False)
    dr = {}
    dr["xT"] = nc.dram_tensor("xT", [D, T], F32, kind="ExternalInput").ap()
    dr["out"] = nc.dram_tensor("out", [D, T], F32, kind="ExternalOutput").ap()
    dr["wu_n"] = nc.dram_tensor("wu_n", [II, D], F32, kind="ExternalInput").ap()
    dr["wg_n"] = nc.dram_tensor("wg_n", [II, D], F32, kind="ExternalInput").ap()
    dr["wd_n"] = nc.dram_tensor("wd_n", [D, II], F32, kind="ExternalInput").ap()
    dr["wuT"] = nc.dram_tensor("wuT", [D, II], F32, kind="ExternalInput").ap()
    dr["wgT"] = nc.dram_tensor("wgT", [D, II], F32, kind="ExternalInput").ap()
    dr["wdT"] = nc.dram_tensor("wdT", [II, D], F32, kind="ExternalInput").ap()
    dr["wcols"] = nc.dram_tensor("wcols", [128, 24], F32, kind="ExternalInput").ap()
    dr["eye"] = nc.dram_tensor("eye", [128, 128], F32, kind="ExternalInput").ap()
    dr["ones"] = nc.dram_tensor("ones", [128, 128], F32, kind="ExternalInput").ap()
    dr["selug"] = nc.dram_tensor("selug", [32, 1024], F32, kind="ExternalInput").ap()
    dr["seldn"] = nc.dram_tensor("seldn", [96, 3072], F32, kind="ExternalInput").ap()
    dr["sp_up"] = nc.dram_tensor("sp_up", [II, T], F32, kind="Internal").ap()
    dr["sp_g"] = nc.dram_tensor("sp_g", [II, T], F32, kind="Internal").ap()
    dr["sp_o"] = nc.dram_tensor("sp_o", [II, T], F32, kind="Internal").ap()
    dr["y_sp"] = nc.dram_tensor("y_sp", [D, T], F32, kind="Internal").ap()
    for i in (1, 2, 3):
        dr[f"cc{i}i"] = nc.dram_tensor(f"cc{i}i", [1, 8], F32, kind="Internal").ap()
        dr[f"cc{i}o"] = nc.dram_tensor(f"cc{i}o", [1, 8], F32, kind="Internal").ap()
    if DBG:
        dr["dbgv"] = nc.dram_tensor("dbgv", [8, 32], F32, kind="ExternalOutput").ap()
        dr["dbgc"] = nc.dram_tensor("dbgc", [128, 16], F32, kind="ExternalOutput").ap()
        dr["dbgs"] = nc.dram_tensor("dbgs", [128, 2304], F32, kind="ExternalOutput").ap()
        dr["dbgt"] = nc.dram_tensor("dbgt", [32, 6144], F32, kind="ExternalOutput").ap()
        dr["dbgw"] = nc.dram_tensor("dbgw", [128, 1024], F32, kind="ExternalOutput").ap()
        dr["dbgz"] = nc.dram_tensor("dbgz", [D, T], F32, kind="ExternalOutput").ap()

    with ExitStack() as es:
        def sbt(name, shape, dt=F32):
            return es.enter_context(nc.sbuf_tensor(name, shape, dt))
        hd = {}
        hd["R1"] = sbt("R1", [128, 16384])
        hd["R2"] = sbt("R2", [128, 16384])
        hd["R3"] = sbt("R3", [128, 12288])
        hd["SNAT"] = sbt("SNAT", [128, 2304])
        hd["SDT"] = sbt("SDT", [128, 2048])
        hd["CONSTS"] = sbt("CONSTS", [128, 6])
        hd["SELUG"] = sbt("SELUG", [32, 1024])
        hd["WCOL"] = sbt("WCOL", [128, 24])
        hd["NCOL"] = sbt("NCOL", [128, 16])
        hd["BOARD"] = sbt("BOARD", [128, 16])
        hd["COLS"] = sbt("COLS", [128, 16])
        hd["SCAL"] = sbt("SCAL", [128, 32])
        hd["ONES"] = sbt("ONES", [128, 128])
        hd["EYE"] = sbt("EYE", [128, 128])
        hd["pp"] = [es.enter_context(nc.psum_tensor(f"pp{i}", [128, 512], F32))
                    for i in range(4)]
        hd["pq"] = [es.enter_context(nc.psum_tensor(f"pq{i}", [128, 1024], F32))
                    for i in range(2)]
        semnames = ("t", "v", "s", "g", "qy", "qs", "qg")
        sems = {k: es.enter_context(nc.semaphore(f"sem_{k}")) for k in semnames}
        block = es.enter_context(nc.Block())

        S = Sched()
        prog(S, dr, hd)

        @block.tensor
        def _(eng):
            S.run("t", eng, sems)

        @block.vector
        def _(eng):
            S.run("v", eng, sems)

        @block.scalar
        def _(eng):
            S.run("s", eng, sems)

        @block.gpsimd
        def _(eng):
            S.run("g", eng, sems)

        @block.sync
        def _(eng):
            S.run("y", eng, sems)

    return nc


def kernel(L0_attn_x_t, L0_attn_w_i, L0_attn_w_f, L0_attn_w_h,
           L0_ffn_w_gate, L0_ffn_w_up, L0_ffn_w_down):
    x = np.asarray(L0_attn_x_t, dtype=np.float32)
    B = x.shape[0]
    wf = np.asarray(L0_attn_w_f, np.float32)
    wi = np.asarray(L0_attn_w_i, np.float32)
    wh = np.asarray(L0_attn_w_h, np.float32)
    wu = np.ascontiguousarray(np.asarray(L0_ffn_w_up, np.float32))
    wg = np.ascontiguousarray(np.asarray(L0_ffn_w_gate, np.float32))
    wd = np.ascontiguousarray(np.asarray(L0_ffn_w_down, np.float32))

    wcols = np.concatenate([wf.reshape(8, 128).T, wi.reshape(8, 128).T,
                            wh.reshape(8, 128).T], axis=1).astype(np.float32)
    eye = np.eye(128, dtype=np.float32)
    ones = np.ones((128, 128), dtype=np.float32)
    selug = (np.arange(32)[:, None] == (np.arange(1024)[None, :] // 32)).astype(np.float32)
    seldn = (np.arange(96)[:, None] == (np.arange(3072)[None, :] // 32)).astype(np.float32)

    shared = dict(
        wu_n=wu, wg_n=wg, wd_n=wd,
        wuT=np.ascontiguousarray(wu.T), wgT=np.ascontiguousarray(wg.T),
        wdT=np.ascontiguousarray(wd.T),
        wcols=np.ascontiguousarray(wcols), eye=eye, ones=ones,
        selug=np.ascontiguousarray(selug), seldn=np.ascontiguousarray(seldn),
    )
    in_maps = [dict(shared, xT=np.ascontiguousarray(x[b].T)) for b in range(B)]

    nc = build_nc()
    res = run_bass_kernel_spmd(nc, in_maps, core_ids=list(range(B)))
    out = np.stack([np.ascontiguousarray(res.results[b]["out"].T) for b in range(B)])
    return out


# version 13
# speedup vs baseline: 1.0351x; 1.0351x over previous
import numpy as np
import concourse.bass as bass
import concourse.mybir as mybir
from concourse.bass_utils import run_bass_kernel_spmd
from contextlib import ExitStack

F32 = mybir.dt.float32
I32 = mybir.dt.int32
AF = mybir.ActivationFunctionType
ALU = mybir.AluOpType
AX = mybir.AxisListType

T = 2048
D = 1024
II = 3072
NJ = D // 128
NTB = T // 512
NIT = II // 128
NHC = D // 128
QMAX = 32767.0
DBG = False
RG = [[0, 1, 2, 3, 4, 5, 6, 7]]


class Sched:
    def __init__(self):
        self.q = {e: [] for e in "tvsgy"}
        self.cnt = {k: 0 for k in ("t", "v", "s", "g", "qy", "qs", "qg")}
        self.hiw = {e: {} for e in "tvsgy"}

    def _wait(self, e, ev):
        assert ev is not None
        sem, val = ev
        if val <= 0 or self.hiw[e].get(sem, -1) >= val:
            return
        self.hiw[e][sem] = val
        self.q[e].append(("w", sem, val))

    def emit(self, e, fn, waits=(), sig=False):
        for ev in waits:
            self._wait(e, ev)
        if sig:
            self.cnt[e] += 1
            self.q[e].append(("i", fn, e))
            return (e, self.cnt[e])
        self.q[e].append(("i", fn, None))
        return None

    def dma(self, e, out, in_, waits=()):
        for ev in waits:
            self._wait(e, ev)
        qk = {"y": "qy", "s": "qs", "g": "qg"}[e]
        self.cnt[qk] += 16
        self.q[e].append(("d", out, in_, qk))
        return (qk, self.cnt[qk])

    def run(self, e, eng, sems):
        for item in self.q[e]:
            if item[0] == "w":
                eng.wait_ge(sems[item[1]], item[2])
            elif item[0] == "i":
                ins = item[1](eng)
                if item[2] is not None:
                    ins.then_inc(sems[item[2]], 1)
            else:
                eng.dma_start(out=item[1], in_=item[2]).then_inc(sems[item[3]], 16)


def prog(S, dr, sb):
    R1, R2, R3 = sb["R1"], sb["R2"], sb["R3"]
    SNAT, SDT, CONSTS = sb["SNAT"], sb["SDT"], sb["CONSTS"]
    SELUG, WCOL, NCOL = sb["SELUG"], sb["WCOL"], sb["NCOL"]
    BOARD, COLS, SCAL = sb["BOARD"], sb["COLS"], sb["SCAL"]
    ONES, EYE = sb["ONES"], sb["EYE"]
    pp, pq = sb["pp"], sb["pq"]
    V = SCAL

    def seg(t_, a, n):
        return t_[:, a:a + n]

    col_state = {}

    def track(colidx, kind, src_ev, src_ap):
        scratch = COLS[:, 10 + (colidx % 3):11 + (colidx % 3)]
        if kind == "min":
            er = S.emit("v", (lambda sa=src_ap, d=scratch: lambda e: e.tensor_reduce(
                d.rearrange("p (b o) -> p b o", o=1),
                sa.rearrange("p (b k) -> p b k", b=1), AX.X, ALU.min))(),
                waits=[src_ev], sig=True)
        else:
            er = S.emit("v", (lambda sa=src_ap, d=scratch, ab=(kind == "absmax"):
                              lambda e: e.reduce_max(
                d.rearrange("p (b o) -> p b o", o=1),
                sa.rearrange("p (b k) -> p b k", b=1), axis=AX.X,
                apply_absolute_value=ab))(), waits=[src_ev], sig=True)
        if colidx not in col_state:
            col_state[colidx] = True
            return S.emit("v", (lambda c=colidx, d=scratch: lambda e: e.tensor_copy(
                COLS[:, c:c + 1], d))(), waits=[er], sig=True)
        alu = ALU.min if kind == "min" else ALU.max
        return S.emit("v", (lambda c=colidx, d=scratch, a=alu: lambda e: e.tensor_tensor(
            COLS[:, c:c + 1], COLS[:, c:c + 1], d, a))(), waits=[er], sig=True)

    c2s_last = [None]

    def col_to_scalar(ecol, colidx, dst_slot):
        w = [ecol, e_eye] + ([c2s_last[0]] if c2s_last[0] else [])
        e_tp = S.emit("t", (lambda c=colidx: lambda e: e.transpose(
            pp[3][0:1, 0:128], COLS[:, c:c + 1], EYE[:]))(), waits=w, sig=True)
        er = S.emit("v", (lambda ds=dst_slot: lambda e: e.reduce_max(
            V[0:1, ds:ds + 1].rearrange("p (b o) -> p b o", o=1),
            pp[3][0:1, 0:128].rearrange("p (b k) -> p b k", b=1), axis=AX.X))(),
            waits=[e_tp], sig=True)
        c2s_last[0] = er
        return er

    def allreduce(tag, e_ready):
        e_in = S.dma("g", dr[f"cc{tag}i"], V[0:1, 0:8], waits=[e_ready])
        e_ar = S.emit("g", (lambda t_=tag: lambda e: e.collective_compute(
            "AllReduce", ALU.max, ins=[dr[f"cc{t_}i"]], outs=[dr[f"cc{t_}o"]],
            replica_groups=RG))(), waits=[("qg", e_in[1])], sig=True)
        e_out = S.dma("g", V[0:1, 8:16], dr[f"cc{tag}o"], waits=[e_ar])
        return ("qg", e_out[1])

    def board_bcast(waits):
        e_b = S.emit("t", lambda e: e.matmul(pp[3][:, 0:16], ONES[0:1, :], V[0:1, 16:32],
                                             start=True, stop=True),
                     waits=list(waits) + [e_ones], sig=True)
        return S.emit("v", lambda e: e.tensor_copy(BOARD[:, 0:16], pp[3][:, 0:16]),
                      waits=[e_b], sig=True)

    e_wcol = S.dma("y", WCOL[:], dr["wcols"])
    e_eye = S.dma("y", EYE[:], dr["eye"])
    e_ones = S.dma("y", ONES[:], dr["ones"])
    e_selug = S.dma("y", SELUG[:], dr["selug"])
    e_ncol = S.emit("v", lambda e: e.tensor_scalar_mul(NCOL[:, 0:16], WCOL[:, 0:16], -1.0),
                    waits=[("qy", e_wcol[1])], sig=True)
    S.emit("v", lambda e: e.memset(SCAL[:], 0.0))
    S.emit("v", lambda e: e.memset(COLS[:], 0.0))
    S.emit("v", lambda e: e.memset(CONSTS[:, 0:1], 1e-5))
    S.emit("v", lambda e: e.memset(CONSTS[:, 1:2], -0.5))
    S.emit("v", lambda e: e.memset(CONSTS[:, 2:3], 65536.0))
    S.emit("v", lambda e: e.memset(CONSTS[:, 3:4], 12582912.0))
    e_init = S.emit("v", lambda e: e.memset(CONSTS[:, 4:5], -12582912.0), sig=True)

    e_pre = [None] * NIT

    def prepass_up(ot):
        buf = seg(R2, (ot % 4) * 1024, 1024)
        w = [e_pre[ot - 4]] if ot >= 4 else []
        eld = S.dma("s", buf, dr["wu_n"][ot * 128:(ot + 1) * 128, :], waits=w)
        e_pre[ot] = S.emit("v", (lambda b=buf, o=ot: lambda e: e.reduce_max(
            seg(SNAT, o * 32, 32).rearrange("p (b o2) -> p b o2", o2=1),
            b.rearrange("p (b k) -> p b k", k=32), axis=AX.X,
            apply_absolute_value=True))(), waits=[("qs", eld[1])], sig=True)

    XB = [seg(R3, 0, 2048), seg(R3, 2048, 2048)]
    WKA = [[seg(R3, 4096, 2048), seg(R3, 6144, 2048)],
           [seg(R3, 8192, 2048), seg(R3, 10240, 2048)]]
    e_diff = [None] * NJ
    e_xA = [None] * NJ
    for j in range(NJ):
        w = [e_diff[j - 2]] if j >= 2 else []
        e_xA[j] = S.dma("y", XB[j % 2], dr["xT"][j * 128:(j + 1) * 128, :], waits=w)
        wa, wb = WKA[j % 2]
        wx = ("qy", e_xA[j][1])
        S.emit("s", (lambda x=XB[j % 2], a=wa, jj=j: lambda e: e.activation(
            a, x, AF.Exp, bias=0.0, scale=NCOL[:, jj:jj + 1]))(),
            waits=[wx, e_ncol] + ([e_diff[j - 2]] if j >= 2 else []))
        S.emit("s", (lambda a=wa: lambda e: e.activation(a, a, AF.Ln, bias=1.0))())
        S.emit("s", (lambda x=XB[j % 2], b=wb, jj=j: lambda e: e.activation(
            b, x, AF.Exp, bias=0.0, scale=NCOL[:, 8 + jj:9 + jj]))())
        e_ln = S.emit("s", (lambda b=wb: lambda e: e.activation(b, b, AF.Ln, bias=1.0))(),
                      sig=True)
        e_diff[j] = S.emit("v", (lambda a=wa, b=wb, jj=j: lambda e: e.tensor_sub(
            seg(R1, jj * 2048, 2048), a, b))(), waits=[e_ln], sig=True)
        for ot in range(j * 3, min(j * 3 + 3, NIT)):
            prepass_up(ot)
    e_snu = S.emit("v", lambda e: e.tensor_scalar(
        seg(SNAT, 0, 768), seg(SNAT, 0, 768), 1.0 / 7.0, 1e-12, ALU.mult, ALU.max),
        waits=[e_pre[-1]], sig=True)

    SETB = [[seg(R2, 4096 + k * 2048, 2048) for k in range(4)],
            [seg(R2, 12288, 2048), seg(R2, 14336, 2048),
             seg(R3, 4096, 2048), seg(R3, 6144, 2048)]]
    e_r = [None] * NJ
    e_stat = [None] * NJ
    e_xB = [None] * NJ
    e_i = [None] * NJ
    e_sq = [None] * NJ
    for j in range(NJ):
        W0, W1, W2, W3 = SETB[j % 2]
        back = [e_r[j - 2], e_stat[j - 2]] if j >= 2 else []
        guard = [e_diff[NJ - 1]] if j % 2 == 1 else []
        wx_extra = [e_diff[j]] if j < 2 else [e_r[j - 2]]
        e_xB[j] = S.dma("y", XB[j % 2], dr["xT"][j * 128:(j + 1) * 128, :],
                        waits=wx_extra + ([e_diff[NJ - 1]] if j < 2 else []))
        wx = ("qy", e_xB[j][1])
        dj = seg(R1, j * 2048, 2048)
        e_i[j] = S.emit("s", (lambda d=dj, o=W0: lambda e: e.activation(
            o, d, AF.Sigmoid))(), waits=[e_diff[j]] + back, sig=True)
        e_sg = S.emit("s", (lambda x=XB[j % 2], o=W1, jj=j: lambda e: e.activation(
            o, x, AF.Sigmoid, bias=0.0, scale=WCOL[:, 16 + jj:17 + jj]))(),
            waits=[wx] + back, sig=True)
        e_f = S.emit("v", (lambda i=W0, o=W2: lambda e: e.tensor_scalar(
            o, i, -1.0, 1.0, ALU.mult, ALU.add))(),
            waits=[e_i[j]] + back + guard, sig=True)
        S.emit("v", (lambda o=W1: lambda e: e.tensor_scalar_min(o, o, 0.5))(),
               waits=[e_sg])
        S.emit("v", (lambda x=XB[j % 2], o=W3, jj=j: lambda e: e.tensor_scalar(
            o, x, WCOL[:, 16 + jj:17 + jj], 0.0, ALU.mult, ALU.max))(),
            waits=[wx] + guard + back)
        S.emit("v", (lambda a=W3, b=W1: lambda e: e.tensor_add(b, a, b))())
        S.emit("v", (lambda i=W0, g=W1: lambda e: e.tensor_mul(g, i, g))())
        e_h = S.emit("v", (lambda f=W2, v2=W1, o=W0: lambda e: e.tensor_tensor_scan(
            o, f, v2, 0.0, ALU.mult, ALU.add))(), sig=True)
        e_r[j] = S.emit("v", (lambda x=XB[j % 2], h=W0, d=dj: lambda e: e.tensor_add(
            d, x, h))(), sig=True)
        e_sq[j] = S.emit("v", (lambda d=dj, o=W3: lambda e: e.tensor_mul(o, d, d))(),
                         sig=True)
        last = None
        for tb in range(NTB):
            S.emit("t", (lambda d=dj, jj=j, tbb=tb: lambda e: e.matmul(
                pp[tbb][0:1, :], ONES[:, 0:1], d[:, tbb * 512:(tbb + 1) * 512],
                start=(jj == 0), stop=(jj == NJ - 1)))(),
                waits=[e_r[j], e_ones])
            last = S.emit("t", (lambda o=W3, jj=j, tbb=tb: lambda e: e.matmul(
                pq[tbb // 2][0:1, (tbb % 2) * 512:(tbb % 2) * 512 + 512],
                ONES[:, 0:1], o[:, tbb * 512:(tbb + 1) * 512],
                start=(jj == 0), stop=(jj == NJ - 1)))(),
                waits=[e_sq[j]], sig=(tb == NTB - 1))
        e_stat[j] = last

    RW_S = seg(R3, 4096, 2048)
    RW_Q = seg(R3, 6144, 2048)
    e_rowc = []
    for tb in range(NTB):
        e_rowc.append(S.emit("v", (lambda tbb=tb: lambda e: e.tensor_copy(
            RW_S[0:1, tbb * 512:(tbb + 1) * 512], pp[tbb][0:1, :]))(),
            waits=[e_stat[NJ - 1]], sig=True))
        e_rowc.append(S.emit("v", (lambda tbb=tb: lambda e: e.tensor_copy(
            RW_Q[0:1, tbb * 512:(tbb + 1) * 512],
            pq[tbb // 2][0:1, (tbb % 2) * 512:(tbb % 2) * 512 + 512]))(), sig=True))
    e_m = S.emit("v", lambda e: e.tensor_scalar_mul(RW_S[0:1, :], RW_S[0:1, :],
                                                    1.0 / D), waits=[e_rowc[-1]], sig=True)
    e_q_ = S.emit("v", lambda e: e.tensor_scalar_mul(RW_Q[0:1, :], RW_Q[0:1, :], 1.0 / D),
                  waits=[e_m], sig=True)
    RW_M = seg(R3, 0, 2048)
    e_mm = S.emit("v", lambda e: e.tensor_mul(RW_M[0:1, :], RW_S[0:1, :], RW_S[0:1, :]),
                  waits=[e_q_], sig=True)
    e_var = S.emit("v", lambda e: e.tensor_sub(RW_Q[0:1, :], RW_Q[0:1, :],
                                               RW_M[0:1, :]), waits=[e_mm], sig=True)
    e_lnr = S.emit("s", lambda e: e.activation(RW_Q[0:1, :], RW_Q[0:1, :],
                                               AF.Ln, bias=CONSTS[0:1, 0:1]), waits=[e_var], sig=True)
    e_istd = S.emit("s", lambda e: e.activation(RW_Q[0:1, :], RW_Q[0:1, :],
                                                AF.Exp, bias=0.0, scale=CONSTS[0:1, 1:2]),
                    waits=[e_lnr], sig=True)
    e_bc = []
    for tb in range(NTB):
        e_mb = S.emit("t", (lambda tbb=tb: lambda e: e.matmul(
            pp[tbb][:, :], ONES[0:1, :], RW_S[0:1, tbb * 512:(tbb + 1) * 512],
            start=True, stop=True))(), waits=[e_istd, e_m] + e_rowc, sig=True)
        e_ib = S.emit("t", (lambda tbb=tb: lambda e: e.matmul(
            pq[tbb // 2][:, (tbb % 2) * 512:(tbb % 2) * 512 + 512], ONES[0:1, :],
            RW_Q[0:1, tbb * 512:(tbb + 1) * 512],
            start=True, stop=True))(), sig=True)
        e_bc.append((e_mb, e_ib))
    e_appj = [None] * NJ
    e_ysp = [None] * NJ
    e_ymax = None
    for j in range(NJ):
        for tb in range(NTB):
            dj = R1[:, j * 2048 + tb * 512: j * 2048 + (tb + 1) * 512]
            yj = R2[:, j * 2048 + tb * 512: j * 2048 + (tb + 1) * 512]
            S.emit("v", (lambda d=dj, y=yj, tbb=tb: lambda e: e.tensor_sub(
                y, d, pp[tbb][:, :]))(),
                waits=[e_bc[tb][0], e_bc[tb][1], e_snu, e_stat[NJ - 1]])
            e_appj[j] = S.emit("v", (lambda y=yj, tbb=tb: lambda e: e.tensor_mul(
                y, y, pq[tbb // 2][:, (tbb % 2) * 512:(tbb % 2) * 512 + 512]))(),
                sig=True)
        e_ymax = track(0, "absmax", e_appj[j], seg(R2, j * 2048, 2048))
        e_ysp[j] = S.dma("s", dr["y_sp"][j * 128:(j + 1) * 128, :],
                         seg(R2, j * 2048, 2048), waits=[e_appj[j]])
    e_appl = e_appj[NJ - 1]

    e_mx = col_to_scalar(e_ymax, 0, 0)
    wcc1 = allreduce(1, e_mx)
    e_sx = S.emit("v", lambda e: e.tensor_scalar(V[0:1, 0:1], V[0:1, 8:9], 1.0 / QMAX,
                                                 1e-12, ALU.mult, ALU.max),
                  waits=[wcc1], sig=True)
    e_st1 = S.emit("v", lambda e: e.tensor_copy(V[0:1, 31:32], V[0:1, 0:1]),
                   waits=[e_sx], sig=True)
    if DBG:
        S.dma("s", dr["dbgv"][0:1, :], V[0:1, 0:32], waits=[e_st1])
    e_rsx = S.emit("v", lambda e: e.reciprocal(V[0:1, 16:17], V[0:1, 0:1]),
                   waits=[e_sx, e_st1], sig=True)
    if DBG:
        S.dma("s", dr["dbgv"][1:2, :], V[0:1, 0:32], waits=[e_rsx])
    e_board = board_bcast([e_rsx, e_appl])
    B_RSX = BOARD[:, 0:1]

    sTu = R1[0:32, 0:3072]
    rTu = R1[0:32, 3072:6144]
    e_tpu = None
    for ot in range(NIT):
        w = [e_snu, e_appl, e_eye] + ([e_tpu] if e_tpu else [])
        e_t_ = S.emit("t", (lambda o=ot: lambda e: e.transpose(
            pp[2][0:32, 0:128], seg(SNAT, o * 32, 32), EYE[:]))(), waits=w, sig=True)
        e_tpu = S.emit("s", (lambda o=ot: lambda e: e.copy(
            sTu[:, o * 128:(o + 1) * 128], pp[2][0:32, 0:128]))(), waits=[e_t_], sig=True)
    e_rtu = S.emit("v", lambda e: e.reciprocal(rTu, sTu), waits=[e_tpu], sig=True)

    ITMP = seg(R3, 0, 2048).bitcast(I32)
    e_yq = [None] * NJ
    for j in range(NJ):
        yj = seg(R2, j * 2048, 2048)
        e_a = S.emit("v", (lambda y=yj: lambda e: e.tensor_scalar(
            ITMP, y, B_RSX, None, ALU.mult))(),
            waits=[e_board, ("qs", e_ysp[j][1]), e_r[NJ - 1], e_stat[NJ - 1]], sig=True)
        e_yq[j] = S.emit("v", (lambda y=yj: lambda e: e.tensor_copy(y, ITMP))(),
                         waits=[e_a], sig=True)
    e_yq_all = e_yq[NJ - 1]

    RAW = [seg(R3, 0, 1024), seg(R3, 1024, 1024)]
    WD = [seg(R3, 2048, 1024), seg(R3, 3072, 1024)]
    STG = [seg(R3, 4096 + k * 512, 512) for k in range(4)]
    QTMP = seg(R3, 6144, 1024).bitcast(I32)
    PRE = [seg(R3, 7168, 1024), seg(R3, 8192, 1024)]

    dbg_ev = [None]

    def p1_weight(wT_h, sp_h, e_scales, trackers, extra, first_waits, dbg_dump_it=None):
        e_deq = [None] * NIT
        e_spill = {}
        e_evac = {}
        e_last_red = None
        e_grp = {}

        def stage_prep(it):
            w = [e_deq[it - 2]] if it >= 2 else list(first_waits)
            e_raw = S.dma("y", RAW[it % 2].rearrange("p (hc c) -> p hc c", hc=NHC),
                          wT_h.rearrange("(hc p) i -> p hc i", p=128)[
                              :, :, it * 128:(it + 1) * 128], waits=w)
            selw = [e_scales, e_selug] + (list(first_waits) if it == 0 else [e_deq[it - 1]])
            lastsel = None
            for hc in range(NHC):
                S.emit("t", (lambda h=hc, i2=it: lambda e: e.matmul(
                    pq[0][:, h * 128:(h + 1) * 128], SELUG[:, h * 128:(h + 1) * 128],
                    sTu[:, i2 * 128:(i2 + 1) * 128], start=True, stop=True))(), waits=selw)
                lastsel = S.emit("t", (lambda h=hc, i2=it: lambda e: e.matmul(
                    pq[1][:, h * 128:(h + 1) * 128], SELUG[:, h * 128:(h + 1) * 128],
                    rTu[:, i2 * 128:(i2 + 1) * 128], start=True, stop=True))(),
                    sig=(hc == NHC - 1))
            e_q = S.emit("v", (lambda i2=it: lambda e: e.tensor_tensor(
                QTMP, RAW[i2 % 2], pq[1][:, :], ALU.mult))(),
                waits=[lastsel, ("qy", e_raw[1])] + (list(first_waits) if it == 0 else []),
                sig=True)
            e_qf = S.emit("v", (lambda i2=it: lambda e: e.tensor_copy(
                RAW[i2 % 2], QTMP))(), waits=[e_q], sig=True)
            wdw = [e_qf]
            if it >= 2:
                wdw.append(e_grp[(it - 2) * NTB + NTB - 1])
            e_deq[it] = S.emit("v", (lambda i2=it: lambda e: e.tensor_tensor(
                WD[i2 % 2], RAW[i2 % 2], pq[0][:, :], ALU.mult))(), waits=wdw, sig=True)

        def stage_mm(it):
            nonlocal e_last_red
            for tb in range(NTB):
                g = it * NTB + tb
                bank = pp[g % 4]
                mmw = [e_deq[it], e_yq_all]
                if g >= 4:
                    mmw.append(e_evac[g - 4])
                else:
                    mmw += list(first_waits)
                lastmm = None
                for hc in range(NHC):
                    lastmm = S.emit("t", (lambda b=bank, i2=it, tbb=tb, h=hc: lambda e:
                                          e.matmul(
                        b[:, :], WD[i2 % 2][:, h * 128:(h + 1) * 128],
                        R2[:, h * 2048 + tbb * 512: h * 2048 + (tbb + 1) * 512],
                        start=(h == 0), stop=(h == NHC - 1)))(),
                        waits=mmw if hc == 0 else (), sig=(hc == NHC - 1))
                e_grp[g] = lastmm
                stg = STG[g % 4]
                evw = [lastmm]
                if g >= 4:
                    evw.append(("qs", e_spill[g - 4][1]))
                else:
                    evw += list(first_waits)
                e_ev = S.emit("s", (lambda b=bank, o=stg: lambda e: e.copy(o, b[:, :]))(),
                              waits=evw, sig=True)
                e_evac[g] = e_ev
                e_spill[g] = S.dma("s", sp_h[it * 128:(it + 1) * 128,
                                            tb * 512:(tb + 1) * 512], stg, waits=[e_ev])
                for colidx, kind in trackers:
                    e_last_red = track(colidx, kind, e_ev, stg)

        stage_prep(0)
        for it in range(NIT):
            if it + 1 < NIT:
                stage_prep(it + 1)
            stage_mm(it)
            if extra and it < len(extra):
                extra[it]()
        return e_last_red, e_evac[NIT * NTB - 1], e_spill

    def mk_prepass_rows(wn_h, snat_off, n_row_tiles, row_width, first_wait):
        e_prev = [None, None]
        e_last = [None]
        cls = []
        nb = 1024 // 32
        for ot in range(n_row_tiles):
            for c in range(row_width // 1024):
                def go(o=ot, c2=c):
                    buf = PRE[(o * (row_width // 1024) + c2) % 2]
                    par = (o * (row_width // 1024) + c2) % 2
                    w = [e_prev[par]] if e_prev[par] else list(first_wait)
                    eld = S.dma("y", buf, wn_h[o * 128:(o + 1) * 128,
                                               c2 * 1024:(c2 + 1) * 1024], waits=w)
                    er = S.emit("v", (lambda b=buf, o2=o, c3=c2: lambda e: e.reduce_max(
                        seg(SNAT, snat_off + o2 * (row_width // 32) + c3 * nb, nb)
                        .rearrange("p (b o3) -> p b o3", o3=1),
                        b.rearrange("p (b k) -> p b k", k=32), axis=AX.X,
                        apply_absolute_value=True))(), waits=[("qy", eld[1])], sig=True)
                    e_prev[par] = er
                    e_last[0] = er
                cls.append(go)

        def fin():
            tot = n_row_tiles * (row_width // 32)
            return S.emit("v", (lambda: lambda e: e.tensor_scalar(
                seg(SNAT, snat_off, tot), seg(SNAT, snat_off, tot),
                1.0 / 7.0, 1e-12, ALU.mult, ALU.max))(), waits=[e_last[0]], sig=True)
        return cls, fin

    pre_g, fin_g = mk_prepass_rows(dr["wg_n"], 768, NIT, 1024, [e_yq_all])
    pre_d, fin_d = mk_prepass_rows(dr["wd_n"], 1536, NJ, 3072, [e_yq_all])

    p0_done = [e_stat[NJ - 1], e_yq_all, e_rtu]
    e_upred, e_upevac, up_spills = p1_weight(dr["wuT"], dr["sp_up"], e_rtu,
                                             [(2, "absmax")], pre_g, p0_done)
    e_sng = fin_g()

    e_tpg = None
    for ot in range(NIT):
        w = [e_sng, e_upevac, e_eye] + ([e_tpg] if e_tpg else [])
        e_t_ = S.emit("t", (lambda o=ot: lambda e: e.transpose(
            pp[2][0:32, 0:128], seg(SNAT, 768 + o * 32, 32), EYE[:]))(), waits=w, sig=True)
        e_tpg = S.emit("s", (lambda o=ot: lambda e: e.copy(
            sTu[:, o * 128:(o + 1) * 128], pp[2][0:32, 0:128]))(), waits=[e_t_], sig=True)
    e_rtg = S.emit("v", lambda e: e.reciprocal(rTu, sTu), waits=[e_tpg], sig=True)

    if DBG:
        S.dma("s", dr["dbgt"], R1[0:32, 0:6144], waits=[e_rtg])
    e_gred, e_gevac, g_spills = p1_weight(dr["wgT"], dr["sp_g"], e_rtg,
                                          [(3, "max"), (4, "min")], pre_d, [e_upevac],
                                          dbg_dump_it=(4 if DBG else None))
    e_snd = fin_d()

    c2s_last[0] = e_gevac
    e1 = col_to_scalar(e_upred, 2, 0)
    e2 = col_to_scalar(e_gred, 3, 1)
    e_neg = S.emit("v", lambda e: e.tensor_scalar_mul(COLS[:, 4:5], COLS[:, 4:5], -1.0),
                   waits=[e_gred], sig=True)
    e3 = col_to_scalar(e_neg, 4, 2)
    e_pack = S.emit("v", lambda e: e.memset(V[0:1, 3:8], 0.0),
                    waits=[e1, e2, e3], sig=True)
    if DBG:
        S.dma("s", dr["dbgv"][2:3, :], V[0:1, 0:32], waits=[e_pack])
        S.dma("s", dr["dbgc"], COLS[:, 0:16], waits=[e_pack])
    wcc2 = allreduce(2, e_pack)
    vl = [wcc2]

    def vop(fn, extra=()):
        ev = S.emit("v", fn, waits=[vl[0]] + list(extra), sig=True)
        vl[0] = ev
        return ev

    e_sxr = vop(lambda e: e.tensor_copy(V[0:1, 7:8], V[0:1, 31:32]))
    vop(lambda e: e.tensor_mul(V[0:1, 1:2], V[0:1, 8:9], V[0:1, 7:8]))
    e_su = vop(lambda e: e.tensor_scalar(V[0:1, 1:2], V[0:1, 1:2], 1.0 / QMAX,
                                         1e-12, ALU.mult, ALU.max))
    e_rsu = vop(lambda e: e.reciprocal(V[0:1, 2:3], V[0:1, 1:2]))
    vop(lambda e: e.tensor_max(V[0:1, 4:5], V[0:1, 9:10], V[0:1, 10:11]))
    vop(lambda e: e.tensor_mul(V[0:1, 4:5], V[0:1, 4:5], V[0:1, 7:8]))
    e_sg_ = vop(lambda e: e.tensor_scalar(V[0:1, 5:6], V[0:1, 4:5], 1.0 / QMAX,
                                          1e-12, ALU.mult, ALU.max))
    e_rsg = vop(lambda e: e.reciprocal(V[0:1, 6:7], V[0:1, 5:6]))
    vop(lambda e: e.tensor_mul(V[0:1, 3:4], V[0:1, 9:10], V[0:1, 7:8]))
    SI = V[0:1, 11:12].bitcast(I32)
    e_g1 = vop(lambda e: e.tensor_scalar(SI, V[0:1, 3:4], V[0:1, 6:7], None, ALU.mult))
    e_g2 = vop(lambda e: e.tensor_copy(V[0:1, 12:13], SI))
    e_si = S.emit("s", lambda e: e.activation(V[0:1, 13:14], V[0:1, 12:13], AF.Sigmoid,
                                              bias=0.0, scale=V[0:1, 5:6]),
                  waits=[e_g2], sig=True)
    e_s1 = vop(lambda e: e.tensor_scalar(SI, V[0:1, 13:14], 65536.0, None, ALU.mult),
               extra=[e_si])
    vop(lambda e: e.tensor_copy(V[0:1, 13:14], SI))
    vop(lambda e: e.tensor_scalar_min(V[0:1, 13:14], V[0:1, 13:14], 65535.0))
    vop(lambda e: e.tensor_mul(V[0:1, 14:15], V[0:1, 12:13], V[0:1, 13:14]))
    vop(lambda e: e.tensor_mul(V[0:1, 14:15], V[0:1, 14:15], V[0:1, 5:6]))
    vop(lambda e: e.tensor_scalar_mul(V[0:1, 14:15], V[0:1, 14:15], 1.0 / 65536.0))
    e_sg2 = vop(lambda e: e.tensor_scalar(V[0:1, 15:16], V[0:1, 14:15], 1.0 / QMAX,
                                          1e-12, ALU.mult, ALU.max))
    vop(lambda e: e.reciprocal(V[0:1, 21:22], V[0:1, 15:16]))
    vop(lambda e: e.tensor_mul(V[0:1, 16:17], V[0:1, 7:8], V[0:1, 6:7]))
    vop(lambda e: e.tensor_copy(V[0:1, 17:18], V[0:1, 5:6]))
    vop(lambda e: e.tensor_mul(V[0:1, 18:19], V[0:1, 5:6], V[0:1, 21:22]))
    vop(lambda e: e.tensor_scalar_mul(V[0:1, 18:19], V[0:1, 18:19], 1.0 / 65536.0))
    vop(lambda e: e.tensor_mul(V[0:1, 19:20], V[0:1, 7:8], V[0:1, 2:3]))
    e_c6 = vop(lambda e: e.tensor_mul(V[0:1, 20:21], V[0:1, 15:16], V[0:1, 1:2]))
    if DBG:
        S.dma("s", dr["dbgv"][3:4, :], V[0:1, 0:32], waits=[e_c6])
    e_board2 = board_bcast([e_c6])
    B_C1, B_SG, B_C2 = BOARD[:, 0:1], BOARD[:, 1:2], BOARD[:, 2:3]
    B_C3, B_C6 = BOARD[:, 3:4], BOARD[:, 4:5]

    GBUF = [seg(R1, 8192, 2048), seg(R1, 10240, 2048)]
    UBUF = [seg(R1, 12288, 2048), seg(R1, 14336, 2048)]
    BUF_A = [seg(R1, 0, 2048), seg(R1, 2048, 2048), seg(R1, 4096, 2048)]
    BUF_B = [seg(R2, 0, 2048), seg(R2, 2048, 2048), seg(R2, 4096, 2048)]
    BUF_A2 = [seg(R2, 6144, 2048), seg(R2, 8192, 2048)]
    BUF_B2 = [seg(R2, 10240, 2048), seg(R2, 12288, 2048)]
    BUF_C = [seg(R1, 6144, 2048), seg(R2, 14336, 2048)]
    BUF_OB = [seg(R3, 8192, 2048), seg(R3, 10240, 2048)]
    TQ1, TQ2 = seg(R3, 0, 2048), seg(R3, 2048, 2048)
    TQC, TQD = seg(R3, 4096, 2048), seg(R3, 6144, 2048)
    NKT = II // 128
    e_tilev = [None] * NKT
    e_osp = [None] * NKT
    e_gcons = [None] * NKT
    e_ucons = [None] * NKT
    e_omax = None
    p1_done = [e_gevac, e_snd, ("qs", g_spills[NIT * NTB - 1][1])]
    st_g1 = [None] * NKT
    st_g2 = [None] * NKT
    st_ld = {}
    e_s2 = [None] * NKT
    e_rA = [None] * NKT
    e_rB = [None] * NKT
    e_rCa = [None] * NKT
    e_rDa = [None] * NKT
    e_rCb = [None] * NKT
    e_rDb = [None] * NKT
    CLAMP = 65535.0 / 65536.0
    MAGIC = 12582912.0
    B_MAG = CONSTS[:, 3:4]

    def p2_red(k):
        nonlocal e_omax
        e_omax = track(5, "absmax", st_g2[k], BUF_OB[k % 2])
        e_tilev[k] = e_omax

    def p2_ld_g(k):
        wg_ = [e_gcons[k - 2]] if k >= 2 else list(p1_done)
        st_ld[k] = S.dma("y", GBUF[k % 2], dr["sp_g"][k * 128:(k + 1) * 128, :],
                         waits=wg_ + [("qs", g_spills[k * NTB + NTB - 1][1])])

    def p2_ld_u(k):
        wu_ = [e_ucons[k - 2]] if k >= 2 else list(p1_done)
        st_ld[k + 100] = S.dma("y", UBUF[k % 2], dr["sp_up"][k * 128:(k + 1) * 128, :],
                               waits=wu_ + [("qs", up_spills[k * NTB + NTB - 1][1])])

    def p2_v1(k):
        B = BUF_B[k % 3]
        S.emit("v", (lambda o=B, c=CLAMP: lambda e: e.tensor_scalar_min(
            o, o, c))(), waits=[e_s2[k]])
        S.emit("v", (lambda i=B, o=TQ2, m=MAGIC: lambda e: e.tensor_scalar(
            o, i, 65536.0, m, ALU.mult, ALU.add))())
        e_rB[k] = S.emit("v", (lambda i=TQ2, o=B, m=-MAGIC: lambda e:
                               e.tensor_scalar_add(o, i, m))(), sig=True)

    def p2_v2(k):
        w = [st_g2[k - 2]] if k >= 2 else list(p1_done)
        e_rCb[k] = S.emit("v", (lambda o=BUF_A2[k % 2], m=-MAGIC: lambda e:
                                e.tensor_scalar_add(o, TQC, m))(),
                          waits=[e_rCa[k]] + w, sig=True)
        e_rDb[k] = S.emit("v", (lambda o=BUF_B2[k % 2], m=-MAGIC: lambda e:
                                e.tensor_scalar_add(o, TQD, m))(),
                          waits=[e_rDa[k]], sig=True)

    def p2_s_a(k):
        w1 = [st_g1[k - 3]] if k >= 3 else list(p1_done)
        e_s1 = S.emit("s", (lambda b=GBUF[k % 2]: lambda e: e.activation(
            TQ1, b, AF.Identity, bias=B_MAG, scale=B_C1))(),
            waits=[("qy", st_ld[k][1]), e_board2], sig=True)
        e_gcons[k] = e_s1
        e_rA[k] = S.emit("s", (lambda o=BUF_A[k % 3]: lambda e: e.activation(
            o, TQ1, AF.Identity, bias=CONSTS[:, 4:5], scale=1.0))(),
            waits=w1, sig=True)
        e_s2[k] = S.emit("s", (lambda a=BUF_A[k % 3], o=BUF_B[k % 3]: lambda e:
                               e.activation(o, a, AF.Sigmoid, bias=0.0,
                                            scale=B_SG))(), sig=True)

    def p2_s_b(k):
        wc = [e_rCb[k - 1]] if k >= 1 else []
        wd = [e_rDb[k - 1]] if k >= 1 else []
        e_rCa[k] = S.emit("s", (lambda c=BUF_C[k % 2]: lambda e: e.activation(
            TQC, c, AF.Identity, bias=B_MAG, scale=B_C2))(),
            waits=[st_g1[k]] + wc, sig=True)
        e_rDa[k] = S.emit("s", (lambda b=UBUF[k % 2]: lambda e: e.activation(
            TQD, b, AF.Identity, bias=B_MAG, scale=B_C3))(),
            waits=[("qy", st_ld[k + 100][1]), e_board2] + wd, sig=True)
        e_ucons[k] = e_rDa[k]

    def p2_g1(k):
        w = [e_rCa[k - 2]] if k >= 2 else list(p1_done)
        st_g1[k] = S.emit("g", (lambda a=BUF_A[k % 3], b=BUF_B[k % 3],
                                o=BUF_C[k % 2]: lambda e: e.tensor_mul(
            o, a, b))(), waits=[e_rA[k], e_rB[k]] + w, sig=True)

    def p2_g2(k):
        back = [e_tilev[k - 2], ("qs", e_osp[k - 2][1])] if k >= 2 else list(p1_done)
        st_g2[k] = S.emit("g", (lambda a=BUF_A2[k % 2], b=BUF_B2[k % 2],
                                o=BUF_OB[k % 2]: lambda e: e.tensor_mul(
            o, a, b))(), waits=[e_rCb[k], e_rDb[k]] + back, sig=True)
        e_osp[k] = S.dma("s", dr["sp_o"][k * 128:(k + 1) * 128, :], BUF_OB[k % 2],
                         waits=[st_g2[k]])

    p2_ld_g(0)
    p2_ld_g(1)
    for k in range(NKT + 6):
        if 1 <= k - 0 and 0 <= k - 1 < NKT:
            p2_v1(k - 1)
        if 0 <= k - 4 < NKT:
            p2_v2(k - 4)
        if 0 <= k - 6 < NKT:
            p2_red(k - 6)
        if 0 <= k < NKT:
            p2_s_a(k)
        if 0 <= k - 3 < NKT:
            p2_s_b(k - 3)
        if 0 <= k - 2 < NKT:
            p2_g1(k - 2)
        if 0 <= k - 5 < NKT:
            p2_g2(k - 5)
        if 0 <= k + 2 < NKT:
            p2_ld_g(k + 2)
        if 0 <= k - 1 < NKT:
            p2_ld_u(k - 1)

    e_mo = col_to_scalar(e_omax, 5, 0)
    wcc3 = allreduce(3, e_mo)
    vl[0] = wcc3
    vop(lambda e: e.tensor_mul(V[0:1, 1:2], V[0:1, 8:9], V[0:1, 20:21]))
    e_so = vop(lambda e: e.tensor_scalar(V[0:1, 1:2], V[0:1, 1:2], 1.0 / QMAX,
                                         1e-12, ALU.mult, ALU.max))
    e_rso = vop(lambda e: e.reciprocal(V[0:1, 2:3], V[0:1, 1:2]))
    e_c7 = vop(lambda e: e.tensor_mul(V[0:1, 3:4], V[0:1, 20:21], V[0:1, 2:3]))
    vop(lambda e: e.tensor_copy(V[0:1, 16:17], V[0:1, 3:4]))
    e_pk3 = vop(lambda e: e.tensor_copy(V[0:1, 17:18], V[0:1, 1:2]))
    if DBG:
        S.dma("s", dr["dbgv"][4:5, :], V[0:1, 0:32], waits=[e_pk3])
    e_board3 = board_bcast([e_pk3])
    B_C7, B_SO = BOARD[:, 0:1], BOARD[:, 1:2]

    sdT = SDT[0:96, 0:1024]
    rdT = SDT[0:96, 1024:2048]
    e_tpd = None
    for ot in range(NJ):
        w = [e_snd, e_eye] + ([e_tpd] if e_tpd else [e_omax])
        e_t_ = S.emit("t", (lambda o=ot: lambda e: e.transpose(
            pp[2][0:96, 0:128], seg(SNAT, 1536 + o * 96, 96), EYE[:]))(), waits=w, sig=True)
        e_tpd = S.emit("s", (lambda o=ot: lambda e: e.copy(
            sdT[:, o * 128:(o + 1) * 128], pp[2][0:96, 0:128]))(), waits=[e_t_], sig=True)
    e_rtd = S.emit("v", lambda e: e.reciprocal(rdT, sdT), waits=[e_tpd], sig=True)

    e_seldn = S.dma("y", R2[0:96, 12288:15360], dr["seldn"],
                    waits=[e_gevac, e_tilev[NKT - 1]])
    OQ = [seg(R2, c * 2048, 2048) for c in range(6)]
    SELDN = R2[0:96, 12288:15360]
    DRAW = [seg(R3, 0, 768), seg(R3, 768, 768)]
    DWD = [seg(R3, 1536, 768), seg(R3, 2304, 768)]
    DQT = seg(R3, 3072, 768).bitcast(I32)
    OIT2 = [seg(R3, 3840, 2048).bitcast(I32), seg(R3, 7936, 2048).bitcast(I32)]
    YP = [seg(R3, 5888, 512), seg(R3, 6400, 512)]
    OST = [seg(R3, 6912, 512), seg(R3, 7424, 512)]

    NPASS, ICB = 4, 6
    e_oq = {}
    e_deqd = {}
    e_z = {}
    zev_order = []
    e_yp_cons = {}
    st_d = {}

    def p3_oq_loads(p):
        for c in range(ICB):
            ic = p * ICB + c
            w = [e_board3, ("qs", e_osp[ic][1])]
            if p > 0:
                w.append(e_z[(p - 1, NJ - 1, NTB - 1)])
            else:
                w.append(e_gevac)
            eld = S.dma("y", OQ[c], dr["sp_o"][ic * 128:(ic + 1) * 128, :], waits=w)
            prev = e_oq.get(ic - 2)
            oit = OIT2[c % 2]
            ei = S.emit("s", (lambda b=OQ[c], o=oit: lambda e: e.activation(
                o, b, AF.Identity, bias=0.0, scale=B_C7))(),
                waits=[("qy", eld[1])] + ([prev] if prev else [e_board3]), sig=True)
            e_oq[ic] = S.emit("v", (lambda b=OQ[c], o=oit: lambda e: e.tensor_copy(
                b, o))(), waits=[ei], sig=True)

    def p3_prep(u):
        p, ho = divmod(u, NJ)
        hb = u % 2
        wsl = [e_deqd[u - 2]] if u >= 2 else [e_rtd]
        eslab = S.dma("y", DRAW[hb].rearrange("p (s c) -> p s c", s=ICB),
                      dr["wdT"].rearrange("(s p) i -> p s i", p=128)[
                          :, p * ICB:(p + 1) * ICB, ho * 128:(ho + 1) * 128],
                      waits=wsl)
        selw = [e_rtd, ("qy", e_seldn[1])] + ([e_deqd[u - 1]] if u >= 1 else [])
        lastsel = None
        for c in range(ICB):
            ic = p * ICB + c
            S.emit("t", (lambda i2=ic, c2=c, h2=ho: lambda e: e.matmul(
                pq[0][:, c2 * 128:(c2 + 1) * 128], SELDN[:, i2 * 128:(i2 + 1) * 128],
                sdT[:, h2 * 128:(h2 + 1) * 128], start=True, stop=True))(), waits=selw)
            lastsel = S.emit("t", (lambda i2=ic, c2=c, h2=ho: lambda e: e.matmul(
                pq[1][:, c2 * 128:(c2 + 1) * 128], SELDN[:, i2 * 128:(i2 + 1) * 128],
                rdT[:, h2 * 128:(h2 + 1) * 128], start=True, stop=True))(),
                sig=(c == ICB - 1))
        e_q = S.emit("v", (lambda b=hb: lambda e: e.tensor_tensor(
            DQT, DRAW[b], pq[1][:, 0:768], ALU.mult))(),
            waits=[lastsel, ("qy", eslab[1])], sig=True)
        e_qf = S.emit("v", (lambda b=hb: lambda e: e.tensor_copy(DRAW[b], DQT))(),
                      waits=[e_q], sig=True)
        dww = [e_qf]
        if u >= 2:
            pv = divmod(u - 2, NJ)
            dww.append(e_z[(pv[0], pv[1], NTB - 1)])
        e_deqd[u] = S.emit("v", (lambda b=hb: lambda e: e.tensor_tensor(
            DWD[b], DRAW[b], pq[0][:, 0:768], ALU.mult))(), waits=dww, sig=True)

    def p3_mm(u):
        p, ho = divmod(u, NJ)
        hb = u % 2
        e_oq_p = e_oq[p * ICB + ICB - 1]
        for tb in range(NTB):
            gi = len(zev_order)
            bank = pp[gi % 2]
            mmw = [e_deqd[u], e_oq_p]
            if gi >= 2:
                mmw.append(zev_order[gi - 2])
            lastmm = None
            for c in range(ICB):
                lastmm = S.emit("t", (lambda b=bank, b2=hb, c2=c, t2=tb: lambda e:
                                      e.matmul(
                    b[:, :], DWD[b2][:, c2 * 128:(c2 + 1) * 128],
                    OQ[c2][:, t2 * 512:(t2 + 1) * 512],
                    start=(c2 == 0), stop=(c2 == ICB - 1)))(),
                    waits=mmw if c == 0 else (), sig=(c == ICB - 1))
            zsl = R1[:, ho * 2048 + tb * 512: ho * 2048 + (tb + 1) * 512]
            if p == 0:
                ez = S.emit("v", (lambda z=zsl, b=bank: lambda e: e.tensor_copy(
                    z, b[:, :]))(), waits=[lastmm, e_omax,
                                           ("qs", e_osp[NKT - 1][1])], sig=True)
            elif p < NPASS - 1:
                ez = S.emit("v", (lambda z=zsl, b=bank: lambda e: e.tensor_add(
                    z, z, b[:, :]))(), waits=[lastmm], sig=True)
            else:
                yw = [e_yp_cons[gi % 2]] if (gi % 2) in e_yp_cons else []
                eyp = S.dma("y", YP[gi % 2],
                            dr["y_sp"][ho * 128:(ho + 1) * 128,
                                       tb * 512:(tb + 1) * 512],
                            waits=yw + [("qs", e_ysp[ho][1])])
                S.emit("v", (lambda z=zsl, b=bank: lambda e: e.tensor_add(
                    z, z, b[:, :]))(), waits=[lastmm])
                ez = S.emit("v", (lambda z=zsl, yp=YP[gi % 2]: lambda e:
                                  e.scalar_tensor_tensor(
                    z, z, B_SO, yp, ALU.mult, ALU.add))(),
                    waits=[("qy", eyp[1]), e_board3], sig=True)
                e_yp_cons[gi % 2] = ez
            e_z[(p, ho, tb)] = ez
            zev_order.append(ez)

    for p in range(NPASS):
        p3_oq_loads(p)
        if p == 0:
            p3_prep(0)
        for ho in range(NJ):
            u = p * NJ + ho
            if u + 1 < NPASS * NJ:
                p3_prep(u + 1)
            p3_mm(u)
    e_z_last = zev_order[-1]

    SQ2 = seg(R3, 0, 2048)
    e_stat2 = None
    for j in range(NJ):
        zj = seg(R1, j * 2048, 2048)
        sqw = [e_z_last] + ([e_stat2] if e_stat2 else [])
        e_sq = S.emit("s", (lambda z=zj, o=SQ2: lambda e: e.activation(
            o, z, AF.Square))(), waits=sqw, sig=True)
        for tb in range(NTB):
            S.emit("t", (lambda z=zj, jj=j, tbb=tb: lambda e: e.matmul(
                pp[tbb][0:1, :], ONES[:, 0:1], z[:, tbb * 512:(tbb + 1) * 512],
                start=(jj == 0), stop=(jj == NJ - 1)))(), waits=[e_z_last])
            e_stat2 = S.emit("t", (lambda o=SQ2, jj=j, tbb=tb: lambda e: e.matmul(
                pq[tbb // 2][0:1, (tbb % 2) * 512:(tbb % 2) * 512 + 512],
                ONES[:, 0:1], o[:, tbb * 512:(tbb + 1) * 512],
                start=(jj == 0), stop=(jj == NJ - 1)))(),
                waits=[e_sq], sig=(tb == NTB - 1))
    RW2_S = seg(R3, 8192, 2048)
    RW2_Q = seg(R3, 10240, 2048)
    e_rowc2 = []
    for tb in range(NTB):
        e_rowc2.append(S.emit("v", (lambda tbb=tb: lambda e: e.tensor_copy(
            RW2_S[0:1, tbb * 512:(tbb + 1) * 512], pp[tbb][0:1, :]))(),
            waits=[e_stat2], sig=True))
        e_rowc2.append(S.emit("v", (lambda tbb=tb: lambda e: e.tensor_copy(
            RW2_Q[0:1, tbb * 512:(tbb + 1) * 512],
            pq[tbb // 2][0:1, (tbb % 2) * 512:(tbb % 2) * 512 + 512]))(), sig=True))
    e_m2 = S.emit("v", lambda e: e.tensor_scalar_mul(RW2_S[0:1, :], RW2_S[0:1, :],
                                                     1.0 / D), waits=[e_rowc2[-1]], sig=True)
    e_q2_ = S.emit("v", lambda e: e.tensor_scalar_mul(RW2_Q[0:1, :], RW2_Q[0:1, :], 1.0 / D),
                   waits=[e_m2], sig=True)
    e_mm2 = S.emit("v", lambda e: e.tensor_mul(SQ2[0:1, :], RW2_S[0:1, :], RW2_S[0:1, :]),
                   waits=[e_q2_], sig=True)
    e_var2 = S.emit("v", lambda e: e.tensor_sub(RW2_Q[0:1, :], RW2_Q[0:1, :],
                                                SQ2[0:1, :]), waits=[e_mm2], sig=True)
    e_lnr2 = S.emit("s", lambda e: e.activation(RW2_Q[0:1, :], RW2_Q[0:1, :],
                                                AF.Ln, bias=CONSTS[0:1, 0:1]), waits=[e_var2], sig=True)
    e_istd2 = S.emit("s", lambda e: e.activation(RW2_Q[0:1, :], RW2_Q[0:1, :],
                                                 AF.Exp, bias=0.0, scale=CONSTS[0:1, 1:2]),
                     waits=[e_lnr2], sig=True)
    e_outs = []
    n_out = 0
    for tb in range(NTB):
        e_mb = S.emit("t", (lambda tbb=tb: lambda e: e.matmul(
            pp[tbb][:, :], ONES[0:1, :], RW2_S[0:1, tbb * 512:(tbb + 1) * 512],
            start=True, stop=True))(), waits=[e_istd2, e_m2] + e_rowc2, sig=True)
        e_ib = S.emit("t", (lambda tbb=tb: lambda e: e.matmul(
            pq[tbb // 2][:, (tbb % 2) * 512:(tbb % 2) * 512 + 512], ONES[0:1, :],
            RW2_Q[0:1, tbb * 512:(tbb + 1) * 512],
            start=True, stop=True))(), sig=True)
        for j in range(NJ):
            zj = R1[:, j * 2048 + tb * 512: j * 2048 + (tb + 1) * 512]
            ost = OST[n_out % 2]
            w_st = [("qs", e_outs[n_out - 2][1])] if n_out >= 2 else []
            S.emit("v", (lambda z=zj, o=ost, tbb=tb: lambda e: e.tensor_sub(
                o, z, pp[tbb][:, :]))(), waits=[e_mb, e_ib] + w_st)
            e_o = S.emit("v", (lambda o=ost, tbb=tb: lambda e: e.tensor_mul(
                o, o, pq[tbb // 2][:, (tbb % 2) * 512:(tbb % 2) * 512 + 512]))(), sig=True)
            e_outs.append(S.dma("s", dr["out"][j * 128:(j + 1) * 128,
                                               tb * 512:(tb + 1) * 512], ost, waits=[e_o]))
            n_out += 1
    if DBG:
        S.dma("s", dr["dbgs"], SNAT[:, 0:2304], waits=[e_outs[-1]])
        for j in range(NJ):
            S.dma("s", dr["dbgz"][j * 128:(j + 1) * 128, :], seg(R1, j * 2048, 2048),
                  waits=[e_outs[-1]])
    S.emit("s", lambda e: e.nop(), waits=[("qs", S.cnt["qs"])])
    S.emit("g", lambda e: e.nop(), waits=[("qs", e_outs[-1][1])])


def build_nc():
    nc = bass.Bass("TRN2", target_bir_lowering=False)
    dr = {}
    dr["xT"] = nc.dram_tensor("xT", [D, T], F32, kind="ExternalInput").ap()
    dr["out"] = nc.dram_tensor("out", [D, T], F32, kind="ExternalOutput").ap()
    dr["wu_n"] = nc.dram_tensor("wu_n", [II, D], F32, kind="ExternalInput").ap()
    dr["wg_n"] = nc.dram_tensor("wg_n", [II, D], F32, kind="ExternalInput").ap()
    dr["wd_n"] = nc.dram_tensor("wd_n", [D, II], F32, kind="ExternalInput").ap()
    dr["wuT"] = nc.dram_tensor("wuT", [D, II], F32, kind="ExternalInput").ap()
    dr["wgT"] = nc.dram_tensor("wgT", [D, II], F32, kind="ExternalInput").ap()
    dr["wdT"] = nc.dram_tensor("wdT", [II, D], F32, kind="ExternalInput").ap()
    dr["wcols"] = nc.dram_tensor("wcols", [128, 24], F32, kind="ExternalInput").ap()
    dr["eye"] = nc.dram_tensor("eye", [128, 128], F32, kind="ExternalInput").ap()
    dr["ones"] = nc.dram_tensor("ones", [128, 128], F32, kind="ExternalInput").ap()
    dr["selug"] = nc.dram_tensor("selug", [32, 1024], F32, kind="ExternalInput").ap()
    dr["seldn"] = nc.dram_tensor("seldn", [96, 3072], F32, kind="ExternalInput").ap()
    dr["sp_up"] = nc.dram_tensor("sp_up", [II, T], F32, kind="Internal").ap()
    dr["sp_g"] = nc.dram_tensor("sp_g", [II, T], F32, kind="Internal").ap()
    dr["sp_o"] = nc.dram_tensor("sp_o", [II, T], F32, kind="Internal").ap()
    dr["y_sp"] = nc.dram_tensor("y_sp", [D, T], F32, kind="Internal").ap()
    for i in (1, 2, 3):
        dr[f"cc{i}i"] = nc.dram_tensor(f"cc{i}i", [1, 8], F32, kind="Internal").ap()
        dr[f"cc{i}o"] = nc.dram_tensor(f"cc{i}o", [1, 8], F32, kind="Internal").ap()
    if DBG:
        dr["dbgv"] = nc.dram_tensor("dbgv", [8, 32], F32, kind="ExternalOutput").ap()
        dr["dbgc"] = nc.dram_tensor("dbgc", [128, 16], F32, kind="ExternalOutput").ap()
        dr["dbgs"] = nc.dram_tensor("dbgs", [128, 2304], F32, kind="ExternalOutput").ap()
        dr["dbgt"] = nc.dram_tensor("dbgt", [32, 6144], F32, kind="ExternalOutput").ap()
        dr["dbgw"] = nc.dram_tensor("dbgw", [128, 1024], F32, kind="ExternalOutput").ap()
        dr["dbgz"] = nc.dram_tensor("dbgz", [D, T], F32, kind="ExternalOutput").ap()

    with ExitStack() as es:
        def sbt(name, shape, dt=F32):
            return es.enter_context(nc.sbuf_tensor(name, shape, dt))
        hd = {}
        hd["R1"] = sbt("R1", [128, 16384])
        hd["R2"] = sbt("R2", [128, 16384])
        hd["R3"] = sbt("R3", [128, 12288])
        hd["SNAT"] = sbt("SNAT", [128, 2304])
        hd["SDT"] = sbt("SDT", [128, 2048])
        hd["CONSTS"] = sbt("CONSTS", [128, 6])
        hd["SELUG"] = sbt("SELUG", [32, 1024])
        hd["WCOL"] = sbt("WCOL", [128, 24])
        hd["NCOL"] = sbt("NCOL", [128, 16])
        hd["BOARD"] = sbt("BOARD", [128, 16])
        hd["COLS"] = sbt("COLS", [128, 16])
        hd["SCAL"] = sbt("SCAL", [128, 32])
        hd["ONES"] = sbt("ONES", [128, 128])
        hd["EYE"] = sbt("EYE", [128, 128])
        hd["pp"] = [es.enter_context(nc.psum_tensor(f"pp{i}", [128, 512], F32))
                    for i in range(4)]
        hd["pq"] = [es.enter_context(nc.psum_tensor(f"pq{i}", [128, 1024], F32))
                    for i in range(2)]
        semnames = ("t", "v", "s", "g", "qy", "qs", "qg")
        sems = {k: es.enter_context(nc.semaphore(f"sem_{k}")) for k in semnames}
        block = es.enter_context(nc.Block())

        S = Sched()
        prog(S, dr, hd)

        @block.tensor
        def _(eng):
            S.run("t", eng, sems)

        @block.vector
        def _(eng):
            S.run("v", eng, sems)

        @block.scalar
        def _(eng):
            S.run("s", eng, sems)

        @block.gpsimd
        def _(eng):
            S.run("g", eng, sems)

        @block.sync
        def _(eng):
            S.run("y", eng, sems)

    return nc


def kernel(L0_attn_x_t, L0_attn_w_i, L0_attn_w_f, L0_attn_w_h,
           L0_ffn_w_gate, L0_ffn_w_up, L0_ffn_w_down):
    x = np.asarray(L0_attn_x_t, dtype=np.float32)
    B = x.shape[0]
    wf = np.asarray(L0_attn_w_f, np.float32)
    wi = np.asarray(L0_attn_w_i, np.float32)
    wh = np.asarray(L0_attn_w_h, np.float32)
    wu = np.ascontiguousarray(np.asarray(L0_ffn_w_up, np.float32))
    wg = np.ascontiguousarray(np.asarray(L0_ffn_w_gate, np.float32))
    wd = np.ascontiguousarray(np.asarray(L0_ffn_w_down, np.float32))

    wcols = np.concatenate([wf.reshape(8, 128).T, wi.reshape(8, 128).T,
                            wh.reshape(8, 128).T], axis=1).astype(np.float32)
    eye = np.eye(128, dtype=np.float32)
    ones = np.ones((128, 128), dtype=np.float32)
    selug = (np.arange(32)[:, None] == (np.arange(1024)[None, :] // 32)).astype(np.float32)
    seldn = (np.arange(96)[:, None] == (np.arange(3072)[None, :] // 32)).astype(np.float32)

    shared = dict(
        wu_n=wu, wg_n=wg, wd_n=wd,
        wuT=np.ascontiguousarray(wu.T), wgT=np.ascontiguousarray(wg.T),
        wdT=np.ascontiguousarray(wd.T),
        wcols=np.ascontiguousarray(wcols), eye=eye, ones=ones,
        selug=np.ascontiguousarray(selug), seldn=np.ascontiguousarray(seldn),
    )
    in_maps = [dict(shared, xT=np.ascontiguousarray(x[b].T)) for b in range(B)]

    nc = build_nc()
    res = run_bass_kernel_spmd(nc, in_maps, core_ids=list(range(B)))
    out = np.stack([np.ascontiguousarray(res.results[b]["out"].T) for b in range(B)])
    return out
